# version 03
# baseline (speedup 1.0000x reference)
import contextlib
import numpy as np
import ml_dtypes
import concourse.bass as bass
import concourse.mybir as mybir
from concourse.bass_utils import run_bass_kernel_spmd

F32 = mybir.dt.float32
BF16 = mybir.dt.bfloat16
AF = mybir.ActivationFunctionType
ALU = mybir.AluOpType
AX = mybir.AxisListType

C = 1024
H = 16
D = 64
TOWN = 2048
TKV = 4096
NCH = 8
LN_EPS = 1e-5
RMS_EPS = 1e-6
FFT_EPS = 1e-8
NEG = -1.0e30
BF = ml_dtypes.bfloat16


class Ctr:
    def __init__(self, sem):
        self.sem = sem
        self.v = 0

    def inc(self, inst, n=1):
        inst.then_inc(self.sem, n)
        self.v += n
        return self.v


def _sems(nc, stack):
    out = {}
    for n in ["ld", "mm", "ve", "ac", "st", "gp"]:
        out[n] = Ctr(stack.enter_context(nc.semaphore("s_" + n)))
    return out


def _sbap(t, off, dims):
    return bass.AP(t.tensor if hasattr(t, "tensor") else t, off, dims)


def build_l0a():
    nc = bass.Bass("TRN2", target_bir_lowering=False)
    xkv = nc.dram_tensor("XKV", [TKV, C], F32, kind="ExternalInput").ap()
    wqkt = nc.dram_tensor("WQKT", [C, 2 * C], F32, kind="ExternalInput").ap()
    wvt = nc.dram_tensor("WVT", [C, C], F32, kind="ExternalInput").ap()
    cosq = nc.dram_tensor("COSQ", [128, TOWN], F32, kind="ExternalInput").ap()
    sinq = nc.dram_tensor("SINQ", [128, TOWN], F32, kind="ExternalInput").ap()
    cosk = nc.dram_tensor("COSK", [128, TKV], F32, kind="ExternalInput").ap()
    sink = nc.dram_tensor("SINK", [128, TKV], F32, kind="ExternalInput").ap()
    qnp = nc.dram_tensor("QNP", [128, 1], F32, kind="ExternalInput").ap()
    knp = nc.dram_tensor("KNP", [128, 1], F32, kind="ExternalInput").ap()
    ident_d = nc.dram_tensor("IDENT", [128, 128], BF16, kind="ExternalInput").ap()
    indcol_d = nc.dram_tensor("INDCOL", [128, 2], F32, kind="ExternalInput").ap()
    indrow_d = nc.dram_tensor("INDROW", [2, 128], F32, kind="ExternalInput").ap()
    eps2_d = nc.dram_tensor("EPS2", [2, 1], F32, kind="ExternalInput").ap()

    o_xn = nc.dram_tensor("XN", [C, TOWN], BF16, kind="ExternalOutput").ap()
    o_q = nc.dram_tensor("Q", [C, TOWN], BF16, kind="ExternalOutput").ap()
    o_k = nc.dram_tensor("K", [C, TKV], BF16, kind="ExternalOutput").ap()
    o_v = nc.dram_tensor("V", [TKV, H * 65], BF16, kind="ExternalOutput").ap()
    o_rk = nc.dram_tensor("RKROWS", [H, TKV], F32, kind="ExternalOutput").ap()
    o_mrc = nc.dram_tensor("MRC", [128, 64], F32, kind="ExternalOutput").ap()

    ctx = contextlib.ExitStack()
    with ctx:
        sb = lambda *a: ctx.enter_context(nc.sbuf_tensor(*a))
        psm = lambda *a: ctx.enter_context(nc.psum_tensor(*a))
        xn_fm = sb("xn_fm", [128, NCH * TKV], BF16)
        wqk_sb = sb("wqk_sb", [128, NCH * 2 * C], BF16)
        wv_sb = sb("wv_sb", [128, NCH * C], BF16)
        cq_sb = sb("cq_sb", [128, TOWN], F32)
        sq_sb = sb("sq_sb", [128, TOWN], F32)
        ck_sb = sb("ck_sb", [128, TKV], F32)
        sk_sb = sb("sk_sb", [128, TKV], F32)
        qnp_sb = sb("qnp_sb", [128, 1], F32)
        knp_sb = sb("knp_sb", [128, 1], F32)
        ident = sb("ident", [128, 128], BF16)
        xt = sb("xt", [128, 2 * C], F32)
        xsq = sb("xsq", [128, C], F32)
        mv = sb("mv", [128, 2 * 8], F32)
        xnt = sb("xnt", [128, 2 * C], BF16)
        mrstage = sb("mrstage", [128, 64], F32)
        sw = sb("sw", [128, 2 * 512], F32)
        t1 = sb("t1", [128, 2 * 512], F32)
        sqq = sb("sqq", [128, 2 * 512], F32)
        qout = sb("qout", [128, 2 * 512], BF16)
        rr = sb("rr", [2, 2 * 512], F32)
        rr2 = sb("rr2", [2, 2 * 512], F32)
        indcol = sb("indcol", [128, 2], F32)
        indrow = sb("indrow", [2, 128], F32)
        eps2 = sb("eps2", [2, 1], F32)
        gapb = sb("gapb", [128, 1], F32)
        vstage = sb("vstage", [128, H * 65], BF16)
        pXN = [psm("pXN0", [128, 1024], BF16), psm("pXN1", [128, 1024], BF16)]
        p1 = [psm("p1a", [128, 512], F32), psm("p1b", [128, 512], F32)]
        prow = psm("prow", [2, 512], F32)
        prep = psm("prep", [128, 512], F32)
        pv4 = psm("pv4", [128, 1024], F32)
        s = _sems(nc, ctx)
        ld, mm, ve, ac, st = s["ld"], s["mm"], s["ve"], s["ac"], s["st"]

        with nc.Block() as block:
            @block.sync
            def _(sync):
                for src, dst in ((cosq, cq_sb), (sinq, sq_sb), (cosk, ck_sb),
                                 (sink, sk_sb), (qnp, qnp_sb), (knp, knp_sb),
                                 (ident_d, ident)):
                    sync.dma_start(out=dst[:, :], in_=src).then_inc(ld.sem, 16)
                    ld.v += 16

            @block.gpsimd
            def _(gpsimd):
                for c in range(NCH):
                    gpsimd.dma_start(out=wqk_sb[:, c * 2 * C:(c + 1) * 2 * C],
                                     in_=wqkt[c * 128:(c + 1) * 128, :]).then_inc(ld.sem, 16)
                    gpsimd.dma_start(out=wv_sb[:, c * C:(c + 1) * C],
                                     in_=wvt[c * 128:(c + 1) * 128, :]).then_inc(ld.sem, 16)
                    ld.v += 32
                for src, dst in ((indcol_d, indcol), (indrow_d, indrow), (eps2_d, eps2)):
                    gpsimd.dma_start(out=dst[:, :], in_=src).then_inc(ld.sem, 16)
                    ld.v += 16

            ldv = ld.v

            @block.vector
            def _(vector):
                vector.wait_ge(ld.sem, ldv)
                vector.tensor_scalar_mul(cq_sb[:, :], cq_sb[:, :], qnp_sb[:, 0:1])
                vector.tensor_scalar_mul(sq_sb[:, :], sq_sb[:, :], qnp_sb[:, 0:1])
                vector.tensor_scalar_mul(ck_sb[:, :], ck_sb[:, :], knp_sb[:, 0:1])
                inst = vector.tensor_scalar_mul(sk_sb[:, :], sk_sb[:, :], knp_sb[:, 0:1])
                ve.inc(inst)
        nc.all_engine_barrier()

        ldb, veb, acb, mmb = ld.v, ve.v, ac.v, mm.v
        with nc.Block() as block:
            @block.sync
            def _(sync):
                for t in range(32):
                    if t >= 2:
                        sync.wait_ge(ve.sem, veb + 2 * (t - 2) + 2)
                    b = t % 2
                    sync.dma_start(out=xt[:, b * C:(b + 1) * C],
                                   in_=xkv[t * 128:(t + 1) * 128, :]).then_inc(ld.sem, 16)
                    ld.v += 16

            @block.vector
            def _(vector):
                for t in range(32):
                    b = t % 2
                    xb = xt[:, b * C:(b + 1) * C]
                    vector.wait_ge(ac.sem, acb + 3 * t + 1)
                    vector.reduce_sum(mv[:, b * 8 + 5:b * 8 + 6], xb, axis=AX.X)
                    vector.reduce_sum(mv[:, b * 8 + 6:b * 8 + 7],
                                      xsq[:, 0:C], axis=AX.X)
                    vector.tensor_scalar_mul(mv[:, b * 8 + 0:b * 8 + 1],
                                             mv[:, b * 8 + 5:b * 8 + 6], 1.0 / C)
                    vector.tensor_scalar_mul(mv[:, b * 8 + 6:b * 8 + 7],
                                             mv[:, b * 8 + 6:b * 8 + 7], 1.0 / C)
                    vector.tensor_tensor(mv[:, b * 8 + 7:b * 8 + 8],
                                         mv[:, b * 8 + 0:b * 8 + 1],
                                         mv[:, b * 8 + 0:b * 8 + 1], ALU.mult)
                    vector.tensor_copy(mrstage[:, 2 * t:2 * t + 1],
                                       mv[:, b * 8 + 0:b * 8 + 1])
                    vector.tensor_sub(mv[:, b * 8 + 1:b * 8 + 2],
                                      mv[:, b * 8 + 6:b * 8 + 7],
                                      mv[:, b * 8 + 7:b * 8 + 8])
                    vector.tensor_copy(gapb[:, 0:1], indcol[:, 0:1])
                    inst = vector.tensor_scalar_add(mv[:, b * 8 + 2:b * 8 + 3],
                                                    mv[:, b * 8 + 1:b * 8 + 2], LN_EPS)
                    ve.inc(inst)
                    vector.wait_ge(ac.sem, acb + 3 * t + 2)
                    vector.reciprocal(mv[:, b * 8 + 4:b * 8 + 5], mv[:, b * 8 + 3:b * 8 + 4])
                    vector.tensor_copy(mrstage[:, 2 * t + 1:2 * t + 2],
                                       mv[:, b * 8 + 3:b * 8 + 4])
                    inst = vector.tensor_scalar(xnt[:, b * C:(b + 1) * C], xb,
                                                mv[:, b * 8 + 0:b * 8 + 1],
                                                mv[:, b * 8 + 4:b * 8 + 5],
                                                ALU.subtract, ALU.mult)
                    ve.inc(inst)

            @block.scalar
            def _(scalar):
                for t in range(32):
                    b = t % 2
                    scalar.wait_ge(ld.sem, ldb + 16 * (t + 1))
                    if t >= 1:
                        scalar.wait_ge(ve.sem, veb + 2 * (t - 1) + 1)
                    inst = scalar.activation(xsq[:, 0:C],
                                             xt[:, b * C:(b + 1) * C], AF.Square)
                    ac.inc(inst)
                    scalar.wait_ge(ve.sem, veb + 2 * t + 1)
                    inst = scalar.activation(mv[:, b * 8 + 3:b * 8 + 4],
                                             mv[:, b * 8 + 2:b * 8 + 3], AF.Sqrt)
                    ac.inc(inst)
                    scalar.wait_ge(mm.sem, mmb + t + 1)
                    dst = _sbap(xn_fm, t * 128, [[NCH * TKV, 128], [TKV, NCH], [1, 128]])
                    src = _sbap(pXN[b], 0, [[1024, 128], [128, NCH], [1, 128]])
                    inst = scalar.copy(dst, src)
                    ac.inc(inst)

            @block.tensor
            def _(tensor):
                for t in range(32):
                    b = t % 2
                    tensor.wait_ge(ve.sem, veb + 2 * t + 2)
                    if t >= 2:
                        tensor.wait_ge(ac.sem, acb + 3 * (t - 2) + 3)
                    for c in range(NCH):
                        inst = tensor.transpose(pXN[b][:, c * 128:(c + 1) * 128],
                                                xnt[:, b * C + c * 128:b * C + (c + 1) * 128],
                                                ident[:, :])
                    mm.inc(inst)

            @block.gpsimd
            def _(gpsimd):
                gpsimd.wait_ge(ve.sem, veb + 64)
                gpsimd.dma_start(out=o_mrc, in_=mrstage[:, :]).then_inc(st.sem, 16)
                st.v += 16
        nc.all_engine_barrier()

        shuffle_mask = []
        for q_ in range(0, 32, 2):
            shuffle_mask += [q_ + 1, q_]

        def qk_phase(part):
            TT = TOWN if part == 0 else TKV
            ntt = TT // 512
            n_i = ntt * NCH
            cosT, sinT = (cq_sb, sq_sb) if part == 0 else (ck_sb, sk_sb)
            o_dst = o_q if part == 0 else o_k
            xoff = TOWN if part == 0 else 0
            ldb, veb, acb, mmb, stb = ld.v, ve.v, ac.v, mm.v, st.v
            pe_per = 3 if part == 0 else 2
            st_per = 32 if part == 1 else 16

            with nc.Block() as block:
                @block.tensor
                def _(tensor):
                    for i in range(n_i):
                        tt, cp = i // NCH, i % NCH
                        b = i % 2
                        if i >= 2:
                            tensor.wait_ge(ve.sem, veb + 3 * (i - 2) + 3)
                        for c in range(NCH):
                            inst = tensor.matmul(
                                p1[b][:, :],
                                wqk_sb[:, c * 2 * C + part * C + cp * 128:
                                       c * 2 * C + part * C + (cp + 1) * 128],
                                xn_fm[:, c * TKV + xoff + tt * 512:
                                      c * TKV + xoff + (tt + 1) * 512],
                                start=(c == 0), stop=(c == NCH - 1))
                        mm.inc(inst)
                        tensor.wait_ge(ac.sem, acb + 2 * i + 1)
                        if i >= 1:
                            tensor.wait_ge(ve.sem, veb + 3 * (i - 1) + 1)
                        inst = tensor.matmul(prow[:, :], indcol[:, 0:2],
                                             sqq[:, b * 512:(b + 1) * 512],
                                             start=True, stop=True)
                        mm.inc(inst)
                        if part == 0:
                            tensor.wait_ge(ve.sem, veb + 3 * i + 2)
                            inst = tensor.matmul(prep[:, :], indrow[:, :],
                                                 rr2[0:2, b * 512:(b + 1) * 512],
                                                 start=True, stop=True)
                            mm.inc(inst)

                @block.scalar
                def _(scalar):
                    for i in range(n_i):
                        b = i % 2
                        scalar.wait_ge(mm.sem, mmb + pe_per * i + 1)
                        inst = scalar.activation(sqq[:, b * 512:(b + 1) * 512],
                                                 p1[b][:, :], AF.Square)
                        ac.inc(inst)
                        scalar.wait_ge(ve.sem, veb + 3 * i + 1)
                        if part == 1 and i >= 2:
                            scalar.wait_ge(st.sem, stb + st_per * (i - 1))
                        inst = scalar.activation(rr2[0:2, b * 512:(b + 1) * 512],
                                                 rr[0:2, b * 512:(b + 1) * 512],
                                                 AF.Sqrt, bias=eps2[0:2, 0:1], scale=1.0 / 64.0)
                        ac.inc(inst)

                @block.vector
                def _(vector):
                    for i in range(n_i):
                        tt, cp = i // NCH, i % NCH
                        b = i % 2
                        sl = slice(b * 512, (b + 1) * 512)
                        tsl = slice(tt * 512, (tt + 1) * 512)
                        vector.wait_ge(mm.sem, mmb + pe_per * i + 2)
                        vector.stream_shuffle(sw[:, sl], p1[b][:, :], shuffle_mask)
                        inst = vector.tensor_copy(rr[0:2, sl], prow[0:2, 0:512])
                        ve.inc(inst)
                        vector.wait_ge(ac.sem, acb + 2 * i + 2)
                        vector.reciprocal(rr2[0:2, sl], rr2[0:2, sl])
                        vector.tensor_tensor(t1[:, sl], p1[b][:, :], cosT[:, tsl], ALU.mult)
                        vector.tensor_tensor(sw[:, sl], sw[:, sl], sinT[:, tsl], ALU.mult)
                        if part == 1:
                            inst = vector.tensor_scalar_mul(rr2[0:2, sl], rr2[0:2, sl], 0.125)
                        else:
                            inst = vector.tensor_copy(gapb[:, 0:1], indcol[:, 0:1])
                        ve.inc(inst)
                        if i >= 2:
                            vector.wait_ge(st.sem, stb + st_per * (i - 1))
                        if part == 0:
                            vector.wait_ge(mm.sem, mmb + 3 * i + 3)
                            vector.tensor_add(t1[:, sl], t1[:, sl], sw[:, sl])
                            vector.tensor_copy(gapb[:, 0:1], indcol[:, 0:1])
                            inst = vector.tensor_tensor(qout[:, sl], t1[:, sl],
                                                        prep[:, :], ALU.mult)
                        else:
                            inst = vector.tensor_tensor(qout[:, sl], t1[:, sl],
                                                        sw[:, sl], ALU.add)
                        ve.inc(inst)

                @block.gpsimd
                def _(gpsimd):
                    for i in range(n_i):
                        tt, cp = i // NCH, i % NCH
                        b = i % 2
                        sl = slice(b * 512, (b + 1) * 512)
                        if part == 1:
                            gpsimd.wait_ge(ve.sem, veb + 3 * i + 2)
                            gpsimd.dma_start(out=o_rk[2 * cp:2 * cp + 2,
                                                      tt * 512:(tt + 1) * 512],
                                             in_=rr2[0:2, sl]).then_inc(st.sem, 16)
                            st.v += 16
                        gpsimd.wait_ge(ve.sem, veb + 3 * i + 3)
                        gpsimd.dma_start(out=o_dst[cp * 128:(cp + 1) * 128,
                                                   tt * 512:(tt + 1) * 512],
                                         in_=qout[:, sl]).then_inc(st.sem, 16)
                        st.v += 16
            nc.all_engine_barrier()

        qk_phase(0)
        qk_phase(1)

        mmb, acb, veb, stb = mm.v, ac.v, ve.v, st.v
        with nc.Block() as block:
            @block.gpsimd
            def _(gpsimd):
                src = _sbap(xn_fm, TOWN, [[NCH * TKV, 128], [TKV, NCH], [1, TOWN]])
                gpsimd.dma_start(out=o_xn.rearrange("(c p) n -> p c n", p=128),
                                 in_=src).then_inc(st.sem, 16)
                st.v += 16
                for t in range(32):
                    gpsimd.wait_ge(ve.sem, veb + t + 1)
                    gpsimd.dma_start(out=o_v[t * 128:(t + 1) * 128, :],
                                     in_=vstage[:, :]).then_inc(st.sem, 16)
                    st.v += 16

            @block.tensor
            def _(tensor):
                for t in range(32):
                    if t >= 1:
                        tensor.wait_ge(ac.sem, acb + t)
                    for c in range(NCH):
                        for j in range(2):
                            inst = tensor.matmul(
                                pv4[:, j * 512:(j + 1) * 512],
                                xn_fm[:, c * TKV + t * 128:c * TKV + (t + 1) * 128],
                                wv_sb[:, c * C + j * 512:c * C + (j + 1) * 512],
                                start=(c == 0), stop=(c == NCH - 1))
                    mm.inc(inst)

            @block.scalar
            def _(scalar):
                for t in range(32):
                    scalar.wait_ge(mm.sem, mmb + t + 1)
                    if t >= 1:
                        scalar.wait_ge(st.sem, stb + 16 + 16 * t)
                    dst = _sbap(vstage, 0, [[H * 65, 128], [65, H], [1, 64]])
                    src = _sbap(pv4, 0, [[1024, 128], [64, H], [1, 64]])
                    inst = scalar.copy(dst, src)
                    ac.inc(inst)

            @block.vector
            def _(vector):
                for t in range(32):
                    vector.wait_ge(ac.sem, acb + t + 1)
                    dst = _sbap(vstage, 64, [[H * 65, 128], [65, H], [1, 1]])
                    inst = vector.memset(dst, 1.0)
                    ve.inc(inst)
        nc.all_engine_barrier()

        with nc.Block() as block:
            @block.gpsimd
            def _(gpsimd):
                gpsimd.wait_ge(st.sem, st.v)
    return nc



def _pe_ops_l0b(nkt):
    ops = [("S", 0), ("S", 1)]
    for kt in range(0, nkt - 2):
        ops.append(("PV", kt))
        ops.append(("S", kt + 2))
    ops += [("PV", nkt - 2), ("PV", nkt - 1), ("rep", -1)]
    return ops


def build_l0b():
    nc = bass.Bass("TRN2", target_bir_lowering=False)
    i_q = nc.dram_tensor("Q", [C, TOWN], BF16, kind="ExternalInput").ap()
    i_k = nc.dram_tensor("K", [C, TKV], BF16, kind="ExternalInput").ap()
    i_v = nc.dram_tensor("V", [TKV, H * 65], BF16, kind="ExternalInput").ap()
    i_rk = nc.dram_tensor("RK", [128, H * 32], F32, kind="ExternalInput").ap()
    i_maskc = nc.dram_tensor("MASKC", [128, 2176], BF16, kind="ExternalInput").ap()
    i_small = nc.dram_tensor("SMALL", [1, 129], F32, kind="ExternalInput").ap()
    o_att = nc.dram_tensor("ATT", [C, TOWN], BF16, kind="ExternalOutput").ap()

    NKT = {0: 24, 1: 32}

    ctx = contextlib.ExitStack()
    with ctx:
        sb = lambda *a: ctx.enter_context(nc.sbuf_tensor(*a))
        psm = lambda *a: ctx.enter_context(nc.psum_tensor(*a))
        q_sb = sb("q_sb", [128, NCH * TOWN], BF16)
        k_sb = sb("k_sb", [128, NCH * TKV], BF16)
        v_sb = sb("v_sb", [128, 32 * H * 65], BF16)
        rk_sb = sb("rk_sb", [128, H * 32], F32)
        maskc = sb("maskc", [128, 2176], BF16)
        small = sb("small", [1, 129], F32)
        onesrow = sb("onesrow", [1, 128], F32)
        mrow = sb("mrow", [1, 8], F32)
        biasb = sb("biasb", [128, 2], F32)
        psb = sb("psb", [128, 2 * 1024], BF16)
        sig = sb("sig", [1, 2048], F32)
        repsb = sb("repsb", [64, 1024], F32)
        att_st = sb("att_st", [128, TOWN], BF16)
        gapb = sb("gapb", [128, 1], F32)
        ps_ = [psm("psA", [128, 1024], F32), psm("psB", [128, 1024], F32)]
        pv = psm("pv", [65, 1024], F32)
        prep2 = psm("prep2", [64, 1024], F32)
        s = _sems(nc, ctx)
        ld, mm, ve, ac, st = s["ld"], s["mm"], s["ve"], s["ac"], s["st"]

        with nc.Block() as block:
            @block.sync
            def _(sync):
                for c in range(NCH):
                    sync.dma_start(out=q_sb[:, c * TOWN:(c + 1) * TOWN],
                                   in_=i_q[c * 128:(c + 1) * 128, :]).then_inc(ld.sem, 16)
                    sync.dma_start(out=k_sb[:, c * TKV:(c + 1) * TKV],
                                   in_=i_k[c * 128:(c + 1) * 128, :]).then_inc(ld.sem, 16)
                    ld.v += 32
                for t in range(32):
                    sync.dma_start(out=v_sb[:, t * H * 65:(t + 1) * H * 65],
                                   in_=i_v[t * 128:(t + 1) * 128, :]).then_inc(ld.sem, 16)
                    ld.v += 16
                sync.dma_start(out=rk_sb[:, :], in_=i_rk).then_inc(ld.sem, 16)
                sync.dma_start(out=maskc[:, :], in_=i_maskc).then_inc(ld.sem, 16)
                sync.dma_start(out=small[:, :], in_=i_small).then_inc(ld.sem, 16)
                ld.v += 48

            ldv = ld.v

            @block.vector
            def _(vector):
                vector.wait_ge(ld.sem, ldv)
                vector.memset(onesrow[:, :], 1.0)
                vector.reduce_max(mrow[0:1, 0:1], small[0:1, 0:64], axis=AX.X,
                                  apply_absolute_value=True)
                vector.reduce_max(mrow[0:1, 1:2], small[0:1, 64:128], axis=AX.X,
                                  apply_absolute_value=True)
                vector.tensor_copy(gapb[0:1, 0:1], small[0:1, 128:129])
                vector.tensor_tensor(mrow[0:1, 2:3], mrow[0:1, 0:1], mrow[0:1, 1:2],
                                     ALU.mult)
                vector.tensor_copy(gapb[0:1, 0:1], small[0:1, 128:129])
                vector.tensor_scalar_mul(mrow[0:1, 3:4], mrow[0:1, 2:3], 8.0)
                vector.tensor_copy(gapb[0:1, 0:1], small[0:1, 128:129])
                vector.tensor_scalar_min(mrow[0:1, 4:5], mrow[0:1, 3:4], 60.0)
                vector.tensor_copy(gapb[0:1, 0:1], small[0:1, 128:129])
                vector.tensor_scalar_mul(mrow[0:1, 5:6], mrow[0:1, 4:5], -1.0)
                vector.tensor_copy(gapb[0:1, 0:1], small[0:1, 128:129])
                inst = vector.tensor_tensor(mrow[0:1, 6:7], mrow[0:1, 5:6],
                                            small[0:1, 128:129], ALU.add)
                ve.inc(inst)

            vev0 = ve.v

            @block.tensor
            def _(tensor):
                tensor.wait_ge(ve.sem, vev0)
                tensor.matmul(ps_[0][:, 0:1], onesrow[0:1, :], mrow[0:1, 5:6],
                              start=True, stop=True)
                inst = tensor.matmul(ps_[0][:, 1:2], onesrow[0:1, :], mrow[0:1, 6:7],
                                     start=True, stop=True)
                mm.inc(inst)

            mmv0 = mm.v

            @block.scalar
            def _(scalar):
                scalar.wait_ge(mm.sem, mmv0)
                inst = scalar.copy(biasb[:, 0:2], ps_[0][:, 0:2])
                ac.inc(inst)
        nc.all_engine_barrier()

        mmb, acb, veb, stb = mm.v, ac.v, ve.v, st.v
        sched = {}
        m, a, v_, s_ = mmb, acb, veb, stb
        jlist = [(h, qb) for h in range(H) for qb in range(2)]
        for (h, qb) in jlist:
            j = (h, qb)
            nkt = NKT[qb]
            for op, kt in _pe_ops_l0b(nkt):
                m += 1
                sched[(j, op, kt)] = m
            for kt in range(nkt):
                a += 1
                sched[(j, "exp", kt)] = a
            a += 1
            sched[(j, "repc")] = a
            for kt in range(nkt):
                v_ += 1
                sched[(j, "mask", kt)] = v_
            v_ += 1
            sched[(j, "sig")] = v_
            v_ += 1
            sched[(j, "att")] = v_
            if h % 2 == 1 and qb == 1:
                s_ += 16
                sched[(h // 2, "dma")] = s_

        def diag_region(qb, kt):
            if kt < 16:
                return None
            ktp = kt - 16
            lo = 128 * ktp - qb * 1024
            off2 = lo + 128
            if off2 <= 0 or lo >= 1024:
                return None
            width = min(1024, off2)
            cs = 1024 - lo
            return width, cs

        def prev_j(j):
            i = jlist.index(j)
            return jlist[i - 1] if i > 0 else None

        with nc.Block() as block:
            @block.tensor
            def _(tensor):
                for (h, qb) in jlist:
                    j = (h, qb)
                    hb, cp = h % 2, h // 2
                    nkt = NKT[qb]
                    for op, kt in _pe_ops_l0b(nkt):
                        if op == "S":
                            b = kt % 2
                            if kt >= 2:
                                tensor.wait_ge(ac.sem, sched[(j, "exp", kt - 2)])
                            else:
                                pj = prev_j(j)
                                if pj is not None:
                                    tensor.wait_ge(ac.sem, sched[(pj, "exp", NKT[pj[1]] - 1)])
                            for jj in range(2):
                                inst = tensor.matmul(
                                    ps_[b][:, jj * 512:(jj + 1) * 512],
                                    k_sb[64 * hb:64 * hb + 64,
                                         cp * TKV + kt * 128:cp * TKV + (kt + 1) * 128],
                                    q_sb[64 * hb:64 * hb + 64,
                                         cp * TOWN + qb * 1024 + jj * 512:
                                         cp * TOWN + qb * 1024 + (jj + 1) * 512],
                                    start=True, stop=True)
                            mm.inc(inst)
                            assert mm.v == sched[(j, "S", kt)]
                        elif op == "PV":
                            b = kt % 2
                            tensor.wait_ge(ve.sem, sched[(j, "mask", kt)])
                            for jj in range(2):
                                inst = tensor.matmul(
                                    pv[:, jj * 512:(jj + 1) * 512],
                                    v_sb[:, kt * H * 65 + 65 * h:kt * H * 65 + 65 * h + 65],
                                    psb[:, b * 1024 + jj * 512:b * 1024 + (jj + 1) * 512],
                                    start=(kt == 0), stop=(kt == nkt - 1))
                            mm.inc(inst)
                            assert mm.v == sched[(j, "PV", kt)]
                        else:
                            tensor.wait_ge(ve.sem, sched[(j, "sig")])
                            for jj in range(2):
                                inst = tensor.matmul(
                                    prep2[:, jj * 512:(jj + 1) * 512],
                                    onesrow[0:1, 0:64],
                                    sig[0:1, 1024 + jj * 512:1024 + (jj + 1) * 512],
                                    start=True, stop=True)
                            mm.inc(inst)
                            assert mm.v == sched[(j, "rep", -1)]

            @block.scalar
            def _(scalar):
                for (h, qb) in jlist:
                    j = (h, qb)
                    nkt = NKT[qb]
                    for kt in range(nkt):
                        b = kt % 2
                        scalar.wait_ge(mm.sem, sched[(j, "S", kt)])
                        if kt >= 2:
                            scalar.wait_ge(mm.sem, sched[(j, "PV", kt - 2)])
                        else:
                            pj = prev_j(j)
                            if pj is not None:
                                scalar.wait_ge(mm.sem, sched[(pj, "PV", NKT[pj[1]] - 1)])
                        bias = biasb[:, 1:2] if kt < 16 else biasb[:, 0:1]
                        inst = scalar.activation(
                            psb[:, b * 1024:(b + 1) * 1024], ps_[b][:, :], AF.Exp,
                            bias=bias, scale=rk_sb[:, h * 32 + kt:h * 32 + kt + 1])
                        ac.inc(inst)
                        assert ac.v == sched[(j, "exp", kt)]
                    scalar.wait_ge(mm.sem, sched[(j, "rep", -1)])
                    inst = scalar.copy(repsb[:, :], prep2[:, :])
                    ac.inc(inst)
                    assert ac.v == sched[(j, "repc")]

            @block.vector
            def _(vector):
                for (h, qb) in jlist:
                    j = (h, qb)
                    hb, cp = h % 2, h // 2
                    nkt = NKT[qb]
                    for kt in range(nkt):
                        b = kt % 2
                        vector.wait_ge(ac.sem, sched[(j, "exp", kt)])
                        dr = diag_region(qb, kt)
                        if dr is not None:
                            width, cs = dr
                            inst = vector.tensor_tensor(
                                psb[:, b * 1024:b * 1024 + width],
                                psb[:, b * 1024:b * 1024 + width],
                                maskc[:, cs:cs + width], ALU.mult)
                        else:
                            inst = vector.tensor_copy(gapb[0:1, 0:1], onesrow[0:1, 0:1])
                        ve.inc(inst)
                        assert ve.v == sched[(j, "mask", kt)]
                    vector.wait_ge(mm.sem, sched[(j, "PV", nkt - 1)])
                    vector.tensor_copy(sig[0:1, 0:1024], pv[64:65, :])
                    vector.tensor_copy(gapb[0:1, 0:1], onesrow[0:1, 0:1])
                    inst = vector.reciprocal(sig[0:1, 1024:2048], sig[0:1, 0:1024])
                    ve.inc(inst)
                    assert ve.v == sched[(j, "sig")]
                    vector.wait_ge(ac.sem, sched[(j, "repc")])
                    if h % 2 == 0 and qb == 0 and h // 2 >= 1:
                        vector.wait_ge(st.sem, sched[(h // 2 - 1, "dma")])
                    inst = vector.tensor_tensor(
                        att_st[64 * hb:64 * hb + 64, qb * 1024:(qb + 1) * 1024],
                        pv[0:64, :], repsb[:, :], ALU.mult)
                    ve.inc(inst)
                    assert ve.v == sched[(j, "att")]

            @block.gpsimd
            def _(gpsimd):
                for cp in range(NCH):
                    gpsimd.wait_ge(ve.sem, sched[((2 * cp + 1, 1), "att")])
                    gpsimd.dma_start(out=o_att[cp * 128:(cp + 1) * 128, :],
                                     in_=att_st[:, :]).then_inc(st.sem, 16)
                    st.v += 16
        mm.v = m
        ac.v = a
        ve.v = v_
        nc.all_engine_barrier()

        with nc.Block() as block:
            @block.gpsimd
            def _(gpsimd):
                gpsimd.wait_ge(st.sem, st.v)
    return nc


def build_l0c():
    nc = bass.Bass("TRN2", target_bir_lowering=False)
    i_att = nc.dram_tensor("ATT", [C, TOWN], BF16, kind="ExternalInput").ap()
    i_xn = nc.dram_tensor("XN", [C, TOWN], BF16, kind="ExternalInput").ap()
    i_mr2 = nc.dram_tensor("MR2", [2, TOWN], F32, kind="ExternalInput").ap()
    i_wot = nc.dram_tensor("WOT", [C, C], F32, kind="ExternalInput").ap()
    i_gw = nc.dram_tensor("GW", [C, 4], F32, kind="ExternalInput").ap()
    i_gb = nc.dram_tensor("GB", [4, 1], F32, kind="ExternalInput").ap()
    i_w1 = nc.dram_tensor("W1", [4, C, C], F32, kind="ExternalInput").ap()
    i_w2 = nc.dram_tensor("W2", [4, C, C], F32, kind="ExternalInput").ap()
    i_prm = {}
    for nm in ("B1", "G1", "BE1", "B2", "G2", "BE2"):
        i_prm[nm] = nc.dram_tensor(nm + "T", [4 * 128, NCH], F32, kind="ExternalInput").ap()
    i_identf = nc.dram_tensor("IDENTF", [128, 128], F32, kind="ExternalInput").ap()
    o_x1 = nc.dram_tensor("X1", [C, TOWN], BF16, kind="ExternalOutput").ap()
    wgate_dram = nc.dram_tensor("WGATE_SCRATCH", [4, TOWN], F32).ap()

    ctx = contextlib.ExitStack()
    with ctx:
        sb = lambda *a: ctx.enter_context(nc.sbuf_tensor(*a))
        psm = lambda *a: ctx.enter_context(nc.psum_tensor(*a))
        xn_sb = sb("xn_sb", [128, NCH * TOWN], BF16)
        u1 = sb("u1", [128, NCH * TOWN], BF16)
        att_sb = ybuf = u1
        acc = sb("acc", [128, NCH * TOWN], BF16)
        u2 = sb("u2", [128, NCH * C], BF16)
        wot_sb = w1sb = u2
        w2sb = sb("w2sb", [128, NCH * C], BF16)
        hbuf = sb("hbuf", [128, NCH * TOWN], BF16)
        gw_sb = sb("gw_sb", [128, NCH * 4], BF16)
        gb_sb = sb("gb_sb", [4, 1], F32)
        sd1row = sb("sd1row", [33, TOWN], F32)
        prm = {nm: sb(nm.lower() + "sb", [128, 4 * NCH], F32)
               for nm in ("B1", "G1", "BE1", "B2", "G2", "BE2")}
        identf = sb("identf", [128, 128], F32)
        sqb = sb("sqb", [128, 4 * 512], BF16)
        bscr = sb("bscr", [128, 2048], F32)
        gscr = sb("gscr", [128, 8], F32)
        rows = sb("rows", [1, 4 * 512], F32)
        gsb = bscr
        wsb4 = bscr
        gsb2 = sb("gsb2", [128, 16 * 4], F32)
        edb = sb("edb", [128, 16], F32)
        wq4 = sb("wq4", [128, 16 * 4], F32)
        wrows3 = sb("wrows3", [128, 2 * TOWN], BF16)
        ones4b = sb("ones4b", [128, 128], BF16)
        wrep_sb = sb("wrep_sb", [128, 512], F32)
        ones4 = sb("ones4", [128, 128], F32)
        onescol = sb("onescol", [128, 1], BF16)
        onesrow = sb("onesrow", [1, 128], F32)
        gapb = sb("gapb", [128, 1], F32)
        py = [psm("py%d" % i, [128, 512], F32) for i in range(4)]
        prow = psm("prow", [128, 1024], F32)
        prepA = psm("prepA", [128, 512], F32)
        prepB = psm("prepB", [128, 512], F32)
        s = _sems(nc, ctx)
        ld, mm, ve, ac, st, gp = (s["ld"], s["mm"], s["ve"], s["ac"], s["st"], s["gp"])

        with nc.Block() as block:
            @block.sync
            def _(sync):
                for c in range(NCH):
                    sync.dma_start(out=xn_sb[:, c * TOWN:(c + 1) * TOWN],
                                   in_=i_xn[c * 128:(c + 1) * 128, :]).then_inc(ld.sem, 16)
                    sync.dma_start(out=att_sb[:, c * TOWN:(c + 1) * TOWN],
                                   in_=i_att[c * 128:(c + 1) * 128, :]).then_inc(ld.sem, 16)
                    ld.v += 32
                sync.dma_start(out=sd1row[0:1, :], in_=i_mr2[0:1, :]).then_inc(ld.sem, 16)
                sync.dma_start(out=sd1row[32:33, :], in_=i_mr2[1:2, :]).then_inc(ld.sem, 16)
                ld.v += 16
                sync.dma_start(out=gb_sb[:, :], in_=i_gb).then_inc(ld.sem, 16)
                sync.dma_start(out=identf[:, :], in_=i_identf).then_inc(ld.sem, 16)
                ld.v += 48

            @block.gpsimd
            def _(gpsimd):
                for c in range(NCH):
                    gpsimd.dma_start(out=wot_sb[:, c * C:(c + 1) * C],
                                     in_=i_wot[c * 128:(c + 1) * 128, :]).then_inc(ld.sem, 16)
                    gpsimd.dma_start(out=gw_sb[:, c * 4:(c + 1) * 4],
                                     in_=i_gw[c * 128:(c + 1) * 128, :]).then_inc(ld.sem, 16)
                    ld.v += 32
                for nm in ("B1", "G1", "BE1", "B2", "G2", "BE2"):
                    for e in range(4):
                        gpsimd.dma_start(out=prm[nm][:, e * NCH:(e + 1) * NCH],
                                         in_=i_prm[nm][e * 128:(e + 1) * 128, :]).then_inc(ld.sem, 16)
                        ld.v += 16

            ldv = ld.v

            @block.vector
            def _(vector):
                vector.wait_ge(ld.sem, ldv)
                vector.memset(onesrow[:, :], 1.0)
                vector.memset(ones4[:, :], 1.0)
                vector.memset(ones4b[:, :], 1.0)
                vector.memset(onescol[:, :], 1.0)
                vector.memset(gapb[:, :], 0.0)
                inst = vector.tensor_scalar_add(sd1row[32:33, :], sd1row[32:33, :], 1.0)
                ve.inc(inst)
        nc.all_engine_barrier()

        def gap(vector):
            vector.tensor_copy(gapb[:, 0:1], identf[:, 0:1])

        mmb, acb, veb = mm.v, ac.v, ve.v
        with nc.Block() as block:
            @block.tensor
            def _(tensor):
                for tt in range(4):
                    if tt > 0:
                        tensor.wait_ge(ve.sem, veb + NCH * tt)
                    tensor.matmul(prepA[:, :], ones4[32:33, :],
                                  sd1row[32:33, tt * 512:(tt + 1) * 512],
                                  start=True, stop=True)
                    inst = tensor.matmul(prepB[:, :], onesrow[0:1, :],
                                         sd1row[0:1, tt * 512:(tt + 1) * 512],
                                         start=True, stop=True)
                    mm.inc(inst)
                    for cp in range(NCH):
                        b = cp % 2
                        if tt * NCH + cp >= 2:
                            tensor.wait_ge(ve.sem, veb + tt * NCH + cp - 1)
                        for c in range(NCH):
                            inst = tensor.matmul(
                                py[b][:, :],
                                wot_sb[:, c * C + cp * 128:c * C + (cp + 1) * 128],
                                att_sb[:, c * TOWN + tt * 512:c * TOWN + (tt + 1) * 512],
                                start=(c == 0), stop=(c == NCH - 1))
                        mm.inc(inst)

            @block.vector
            def _(vector):
                for tt in range(4):
                    for cp in range(NCH):
                        b = cp % 2
                        vector.wait_ge(mm.sem, mmb + 9 * tt + 2 + cp)
                        xsl = xn_sb[:, cp * TOWN + tt * 512:cp * TOWN + (tt + 1) * 512]
                        asl = acc[:, cp * TOWN + tt * 512:cp * TOWN + (tt + 1) * 512]
                        vector.tensor_tensor(bscr[:, b * 512:(b + 1) * 512],
                                             xsl, prepA[:, :], ALU.mult)
                        gap(vector)
                        vector.tensor_tensor(bscr[:, b * 512:(b + 1) * 512],
                                             bscr[:, b * 512:(b + 1) * 512],
                                             prepB[:, :], ALU.add)
                        gap(vector)
                        inst = vector.tensor_tensor(asl, bscr[:, b * 512:(b + 1) * 512],
                                                    py[b][:, :], ALU.add)
                        ve.inc(inst)
        nc.all_engine_barrier()

        mmb, acb, veb, stb = mm.v, ac.v, ve.v, st.v
        with nc.Block() as block:
            @block.tensor
            def _(tensor):
                for tt in range(4):
                    if tt > 0:
                        tensor.wait_ge(ac.sem, acb + 7 * tt)
                    pg = py[0][0:4, 0:512]
                    for c in range(NCH):
                        inst = tensor.matmul(pg, gw_sb[:, c * 4:(c + 1) * 4],
                                             xn_sb[:, c * TOWN + tt * 512:
                                                   c * TOWN + (tt + 1) * 512],
                                             start=(c == 0), stop=(c == NCH - 1))
                    mm.inc(inst)
                    tensor.wait_ge(ac.sem, acb + 7 * tt + 1)
                    for sdx in range(4):
                        inst = tensor.transpose(
                            py[1][0:128, 4 * sdx:4 * sdx + 4],
                            gsb[0:4, tt * 512 + 128 * sdx:tt * 512 + 128 * (sdx + 1)],
                            identf[0:4, 0:4])
                    mm.inc(inst)
                    tensor.wait_ge(ve.sem, veb + 8 * tt + 8)
                    for sdx in range(4):
                        inst = tensor.transpose(
                            py[0][0:4, 128 * sdx:128 * (sdx + 1)],
                            wq4[:, (4 * tt + sdx) * 4:(4 * tt + sdx) * 4 + 4],
                            identf[:, :])
                    mm.inc(inst)

            @block.scalar
            def _(scalar):
                for tt in range(4):
                    scalar.wait_ge(mm.sem, mmb + 3 * tt + 1)
                    inst = scalar.activation(gsb[0:4, tt * 512:(tt + 1) * 512],
                                             py[0][0:4, 0:512], AF.Identity,
                                             bias=gb_sb[0:4, 0:1])
                    ac.inc(inst)
                    scalar.wait_ge(mm.sem, mmb + 3 * tt + 2)
                    inst = scalar.copy(gsb2[:, 16 * tt:16 * tt + 16], py[1][0:128, 0:16])
                    ac.inc(inst)
                    for sdx in range(4):
                        scalar.wait_ge(ve.sem, veb + 8 * tt + 2 * sdx + 1)
                        inst = scalar.activation(edb[:, sdx:sdx + 1],
                                                 edb[:, 4 + sdx:5 + sdx], AF.Exp)
                        ac.inc(inst)
                    scalar.wait_ge(mm.sem, mmb + 3 * tt + 3)
                    inst = scalar.copy(wsb4[0:4, tt * 512:(tt + 1) * 512],
                                       py[0][0:4, 0:512])
                    ac.inc(inst)

            @block.vector
            def _(vector):
                for tt in range(4):
                    for sdx in range(4):
                        g4 = gsb2[:, 16 * tt + 4 * sdx:16 * tt + 4 * (sdx + 1)]
                        w4 = wq4[:, (4 * tt + sdx) * 4:(4 * tt + sdx) * 4 + 4]
                        vector.wait_ge(ac.sem, acb + 7 * tt + 2)
                        vector.reduce_max(edb[:, 8:9], g4, axis=AX.X)
                        gap(vector)
                        vector.tensor_single_scalar(w4, g4, edb[:, 8:9], ALU.is_ge)
                        gap(vector)
                        vector.scalar_tensor_tensor(gscr[:, 0:4], w4, NEG, g4,
                                                    ALU.mult, ALU.add)
                        gap(vector)
                        vector.reduce_max(edb[:, 9:10], gscr[:, 0:4], axis=AX.X)
                        gap(vector)
                        inst = vector.tensor_sub(edb[:, 4 + sdx:5 + sdx],
                                                 edb[:, 9:10], edb[:, 8:9])
                        ve.inc(inst)
                        vector.tensor_single_scalar(gscr[:, 4:8], gscr[:, 0:4],
                                                    edb[:, 9:10], ALU.is_ge)
                        vector.wait_ge(ac.sem, acb + 7 * tt + 3 + sdx)
                        vector.tensor_scalar_add(edb[:, 10:11], edb[:, sdx:sdx + 1], 1.0)
                        gap(vector)
                        vector.reciprocal(edb[:, 11:12], edb[:, 10:11])
                        gap(vector)
                        vector.tensor_tensor(edb[:, 12:13], edb[:, sdx:sdx + 1],
                                             edb[:, 11:12], ALU.mult)
                        vector.tensor_single_scalar(w4, w4, edb[:, 11:12], ALU.mult)
                        gap(vector)
                        vector.tensor_single_scalar(gscr[:, 4:8], gscr[:, 4:8],
                                                    edb[:, 12:13], ALU.mult)
                        gap(vector)
                        inst = vector.tensor_add(w4, w4, gscr[:, 4:8])
                        ve.inc(inst)

            @block.gpsimd
            def _(gpsimd):
                gpsimd.wait_ge(ac.sem, acb + 28)
                gpsimd.dma_start(out=wgate_dram, in_=wsb4[0:4, :]).then_inc(st.sem, 16)
                st.v += 16
                gpsimd.wait_ge(st.sem, st.v)
                for e in range(4):
                    if e < 3:
                        dst = wrows3[32 * e:32 * e + 1, 0:TOWN]
                    else:
                        dst = wrows3[0:1, TOWN:2 * TOWN]
                    gpsimd.dma_start(out=dst, in_=wgate_dram[e:e + 1, :]).then_inc(ld.sem, 16)
                    ld.v += 16
                gp.inc(gpsimd.engine_nop())
        ac.v = acb + 28
        ve.v = veb + 32
        mm.v = mmb + 12
        nc.all_engine_barrier()

        LNE = LN_EPS
        events = []
        for e in range(4):
            events.append(("gp", ("wload", e)))
            for l in range(2):
                for h2 in range(2):
                    tts = (2 * h2, 2 * h2 + 1)
                    for cp in range(NCH):
                        events.append(("pe", ("main", e, l, h2, cp)))
                        if cp >= 1:
                            events.append(("pe", ("cols", e, l, h2, cp - 1)))
                        for tt in tts:
                            events.append(("ac", ("evac", e, l, cp, tt)))
                            events.append(("ve", ("sq", e, l, cp, tt)))
                    events.append(("pe", ("cols", e, l, h2, NCH - 1)))
                    for tt in tts:
                        events.append(("ve", ("rows1", e, l, tt)))
                        events.append(("ac", ("sqrt", e, l, tt)))
                        events.append(("ve", ("rows2", e, l, tt)))
                        if l == 1:
                            events.append(("pe", ("wrep", e, tt)))
                            events.append(("ac", ("wrepc", e, tt)))
                        events.append(("pe", ("reps", e, l, tt)))
                        for cp in range(NCH):
                            events.append(("ve", ("t12", e, l, tt, cp)))
                            events.append(("ac", ("act", e, l, tt, cp)))
                            if l == 1:
                                events.append(("ve", ("comb", e, l, tt, cp)))

        mmb, acb, veb, ldb = mm.v, ac.v, ve.v, ld.v
        sched = {}
        ctrs = {"pe": mm.v, "ac": ac.v, "ve": ve.v, "gp": ld.v}
        for eng, tag in events:
            if eng == "gp":
                ctrs["gp"] += 16 * 16
            else:
                ctrs[eng] += 1
            sched[tag] = ctrs[eng]

        def esrc(l):
            return xn_sb if l == 0 else hbuf

        def wsrc(l):
            return w1sb if l == 0 else w2sb

        with nc.Block() as block:
            @block.gpsimd
            def _(gpsimd):
                for eng, tag in events:
                    if eng != "gp":
                        continue
                    e = tag[1]
                    if e > 0:
                        gpsimd.wait_ge(mm.sem, sched[("main", e - 1, 1, 1, NCH - 1)])
                    for c in range(NCH):
                        gpsimd.dma_start(out=w1sb[:, c * C:(c + 1) * C],
                                         in_=i_w1[e, c * 128:(c + 1) * 128, :]).then_inc(ld.sem, 16)
                        gpsimd.dma_start(out=w2sb[:, c * C:(c + 1) * C],
                                         in_=i_w2[e, c * 128:(c + 1) * 128, :]).then_inc(ld.sem, 16)
                        ld.v += 32

            @block.tensor
            def _(tensor):
                for eng, tag in events:
                    if eng != "pe":
                        continue
                    kind = tag[0]
                    if kind == "main":
                        _, e, l, h2, cp = tag
                        tts = (2 * h2, 2 * h2 + 1)
                        tensor.wait_ge(ld.sem, sched[("wload", e)])
                        if cp == 0:
                            if h2 == 1:
                                tensor.wait_ge(ac.sem, sched[("act", e, l, 1, NCH - 1)])
                            elif l == 1:
                                tensor.wait_ge(ac.sem, sched[("act", e, 0, 3, NCH - 1)])
                            elif e > 0:
                                tensor.wait_ge(ve.sem, sched[("comb", e - 1, 1, 3, NCH - 1)])
                        else:
                            tensor.wait_ge(ac.sem, sched[("evac", e, l, cp - 1, tts[1])])
                        wsb = wsrc(l)
                        src = esrc(l)
                        for c in range(NCH):
                            for tt in tts:
                                inst = tensor.matmul(
                                    py[tt][:, :],
                                    wsb[:, c * C + cp * 128:c * C + (cp + 1) * 128],
                                    src[:, c * TOWN + tt * 512:c * TOWN + (tt + 1) * 512],
                                    start=(c == 0), stop=(c == NCH - 1))
                        mm.inc(inst)
                        assert mm.v == sched[tag]
                    elif kind == "cols":
                        _, e, l, h2, cp = tag
                        tts = (2 * h2, 2 * h2 + 1)
                        tensor.wait_ge(ve.sem, sched[("sq", e, l, cp, tts[1])])
                        for tt in tts:
                            p32 = 32 * (tt % 2)
                            ysl = ybuf[:, cp * TOWN + tt * 512:cp * TOWN + (tt + 1) * 512]
                            inst = tensor.matmul(prow[p32:p32 + 1, 0:512],
                                                 onescol[:, 0:1], ysl,
                                                 start=(cp == 0), stop=(cp == NCH - 1),
                                                 tile_position=(0, p32))
                            inst = tensor.matmul(prow[p32:p32 + 1, 512:1024],
                                                 onescol[:, 0:1],
                                                 sqb[:, tt * 512:(tt + 1) * 512],
                                                 start=(cp == 0), stop=(cp == NCH - 1),
                                                 tile_position=(0, p32))
                        mm.inc(inst)
                        assert mm.v == sched[tag]
                    elif kind == "wrep":
                        _, e, tt = tag
                        if tt > 0:
                            tensor.wait_ge(ve.sem, sched[("comb", e, 1, tt - 1, NCH - 1)])
                        elif e > 0:
                            tensor.wait_ge(ve.sem, sched[("comb", e - 1, 1, 3, NCH - 1)])
                        wb = 32 * e if e < 3 else 0
                        wo = 0 if e < 3 else TOWN
                        inst = tensor.matmul(
                            prepB[:, :], ones4b[wb:wb + 1, :],
                            wrows3[wb:wb + 1, wo + tt * 512:wo + (tt + 1) * 512],
                            start=True, stop=True)
                        mm.inc(inst)
                        assert mm.v == sched[tag]
                    elif kind == "reps":
                        _, e, l, tt = tag
                        tensor.wait_ge(ve.sem, sched[("rows2", e, l, tt)])
                        if tt > 0:
                            tensor.wait_ge(ve.sem, sched[("t12", e, l, tt - 1, NCH - 1)])
                        elif l == 1:
                            tensor.wait_ge(ve.sem, sched[("t12", e, 0, 3, NCH - 1)])
                        elif e > 0:
                            tensor.wait_ge(ve.sem, sched[("t12", e - 1, 1, 3, NCH - 1)])
                        if l == 1:
                            tensor.wait_ge(ac.sem, sched[("wrepc", e, tt)])
                        tensor.matmul(prepA[:, :], onesrow[0:1, :],
                                      rows[0:1, 3 * 512:3 * 512 + 512],
                                      start=True, stop=True)
                        inst = tensor.matmul(prepB[:, :], onesrow[0:1, :],
                                             rows[0:1, 1 * 512:1 * 512 + 512],
                                             start=True, stop=True)
                        mm.inc(inst)
                        assert mm.v == sched[tag]

            @block.scalar
            def _(scalar):
                for eng, tag in events:
                    if eng != "ac":
                        continue
                    kind = tag[0]
                    if kind == "evac":
                        _, e, l, cp, tt = tag
                        scalar.wait_ge(mm.sem, sched[("main", e, l, tt // 2, cp)])
                        bcol = prm["B1" if l == 0 else "B2"][:, e * NCH + cp:e * NCH + cp + 1]
                        inst = scalar.activation(
                            ybuf[:, cp * TOWN + tt * 512:cp * TOWN + (tt + 1) * 512],
                            py[tt][:, :], AF.Identity, bias=bcol)
                        ac.inc(inst)
                        assert ac.v == sched[tag]
                    elif kind == "sqrt":
                        _, e, l, tt = tag
                        scalar.wait_ge(ve.sem, sched[("rows1", e, l, tt)])
                        inst = scalar.activation(rows[0:1, 2 * 512:2 * 512 + 512],
                                                 rows[0:1, 1 * 512:1 * 512 + 512], AF.Sqrt)
                        ac.inc(inst)
                        assert ac.v == sched[tag]
                    elif kind == "wrepc":
                        _, e, tt = tag
                        scalar.wait_ge(mm.sem, sched[("wrep", e, tt)])
                        inst = scalar.copy(wrep_sb[:, :], prepB[:, :])
                        ac.inc(inst)
                        assert ac.v == sched[tag]
                    elif kind == "act":
                        _, e, l, tt, cp = tag
                        b = cp % 2
                        scalar.wait_ge(ve.sem, sched[("t12", e, l, tt, cp)])
                        if l == 0:
                            inst = scalar.activation(
                                hbuf[:, cp * TOWN + tt * 512:cp * TOWN + (tt + 1) * 512],
                                bscr[:, b * 512:(b + 1) * 512], AF.Relu,
                                bias=prm["BE1"][:, e * NCH + cp:e * NCH + cp + 1],
                                scale=prm["G1"][:, e * NCH + cp:e * NCH + cp + 1])
                        else:
                            if cp >= 2:
                                scalar.wait_ge(ve.sem, sched[("comb", e, l, tt, cp - 2)])
                            inst = scalar.activation(
                                bscr[:, 1024 + b * 512:1024 + (b + 1) * 512],
                                bscr[:, b * 512:(b + 1) * 512], AF.Identity,
                                bias=prm["BE2"][:, e * NCH + cp:e * NCH + cp + 1],
                                scale=prm["G2"][:, e * NCH + cp:e * NCH + cp + 1])
                        ac.inc(inst)
                        assert ac.v == sched[tag]

            @block.vector
            def _(vector):
                for eng, tag in events:
                    if eng != "ve":
                        continue
                    kind = tag[0]
                    if kind == "sq":
                        _, e, l, cp, tt = tag
                        vector.wait_ge(ac.sem, sched[("evac", e, l, cp, tt)])
                        if cp >= 1:
                            vector.wait_ge(mm.sem, sched[("cols", e, l, tt // 2, cp - 1)])
                        ysl = ybuf[:, cp * TOWN + tt * 512:cp * TOWN + (tt + 1) * 512]
                        inst = vector.tensor_tensor(
                            sqb[:, tt * 512:(tt + 1) * 512],
                            ysl, ysl, ALU.mult)
                        ve.inc(inst)
                        assert ve.v == sched[tag]
                    elif kind == "rows1":
                        _, e, l, tt = tag
                        vector.wait_ge(mm.sem, sched[("cols", e, l, tt // 2, NCH - 1)])
                        R = 512
                        p32 = 32 * (tt % 2)
                        vector.tensor_scalar_mul(rows[0:1, 0:R],
                                                 prow[p32:p32 + 1, 0:512], 1.0 / C)
                        vector.tensor_scalar_mul(rows[0:1, R:2 * R],
                                                 prow[p32:p32 + 1, 512:1024], 1.0 / C)
                        vector.tensor_tensor(rows[0:1, 2 * R:3 * R], rows[0:1, 0:R],
                                             rows[0:1, 0:R], ALU.mult)
                        gap(vector)
                        vector.tensor_sub(rows[0:1, R:2 * R], rows[0:1, R:2 * R],
                                          rows[0:1, 2 * R:3 * R])
                        gap(vector)
                        inst = vector.tensor_scalar_add(rows[0:1, R:2 * R],
                                                        rows[0:1, R:2 * R], LNE)
                        ve.inc(inst)
                        assert ve.v == sched[tag]
                    elif kind == "rows2":
                        _, e, l, tt = tag
                        R = 512
                        vector.wait_ge(ac.sem, sched[("sqrt", e, l, tt)])
                        vector.reciprocal(rows[0:1, 3 * R:4 * R], rows[0:1, 2 * R:3 * R])
                        gap(vector)
                        inst = vector.tensor_tensor(rows[0:1, R:2 * R],
                                                    rows[0:1, 0:R],
                                                    rows[0:1, 3 * R:4 * R], ALU.mult)
                        ve.inc(inst)
                        assert ve.v == sched[tag]
                    elif kind == "t12":
                        _, e, l, tt, cp = tag
                        b = cp % 2
                        vector.wait_ge(mm.sem, sched[("reps", e, l, tt)])
                        if cp >= 2:
                            vector.wait_ge(ac.sem, sched[("act", e, l, tt, cp - 2)])
                        ysl = ybuf[:, cp * TOWN + tt * 512:cp * TOWN + (tt + 1) * 512]
                        vector.tensor_tensor(bscr[:, b * 512:(b + 1) * 512],
                                             ysl, prepA[:, :], ALU.mult)
                        gap(vector)
                        inst = vector.tensor_tensor(bscr[:, b * 512:(b + 1) * 512],
                                                    bscr[:, b * 512:(b + 1) * 512],
                                                    prepB[:, :], ALU.subtract)
                        ve.inc(inst)
                        assert ve.v == sched[tag]
                    elif kind == "comb":
                        _, e, l, tt, cp = tag
                        b = cp % 2
                        vector.wait_ge(ac.sem, sched[("act", e, l, tt, cp)])
                        vector.wait_ge(ac.sem, sched[("wrepc", e, tt)])
                        asl = acc[:, cp * TOWN + tt * 512:cp * TOWN + (tt + 1) * 512]
                        vector.tensor_tensor(bscr[:, b * 512:(b + 1) * 512],
                                             bscr[:, 1024 + b * 512:1024 + (b + 1) * 512],
                                             wrep_sb[:, :], ALU.mult)
                        gap(vector)
                        inst = vector.tensor_tensor(asl, asl,
                                                    bscr[:, b * 512:(b + 1) * 512], ALU.add)
                        ve.inc(inst)
                        assert ve.v == sched[tag]
        nc.all_engine_barrier()

        with nc.Block() as block:
            @block.gpsimd
            def _(gpsimd):
                for c in range(NCH):
                    gpsimd.dma_start(out=o_x1[c * 128:(c + 1) * 128, :],
                                     in_=acc[:, c * TOWN:(c + 1) * TOWN]).then_inc(st.sem, 16)
                    st.v += 16
                gpsimd.wait_ge(st.sem, st.v)
    return nc


def _unorm_act0(scalar, psrc, nsc, ac):
    scalar.activation(nsc[:, 0:511], psrc[:, 1:512], AF.Square)
    scalar.activation(nsc[:, 512:1023], psrc[:, 1023:512:-1], AF.Square)
    inst = scalar.copy(nsc[:, 1536:1538], psrc[:, 0:1024:512])
    ac.inc(inst)


def _unorm_dve1(vector, nsc, ve):
    inst = vector.tensor_add(nsc[:, 0:511], nsc[:, 0:511], nsc[:, 512:1023])
    ve.inc(inst)


def _unorm_act1(scalar, nsc, ac):
    scalar.activation(nsc[:, 512:1023], nsc[:, 0:511], AF.Sqrt)
    inst = scalar.activation(nsc[:, 1538:1540], nsc[:, 1536:1538], AF.Abs)
    ac.inc(inst)


def _unorm_post(vector, psrc, dst, nsc, ve):
    re = psrc[:, 1:512]
    imr = psrc[:, 1023:512:-1]
    sp = psrc[:, 0:1024:512]
    vector.tensor_scalar_add(nsc[:, 512:1023], nsc[:, 512:1023], FFT_EPS)
    vector.tensor_scalar_add(nsc[:, 1538:1540], nsc[:, 1538:1540], FFT_EPS)
    vector.reciprocal(nsc[:, 0:511], nsc[:, 512:1023])
    vector.reciprocal(nsc[:, 1540:1542], nsc[:, 1538:1540])
    vector.tensor_tensor(dst[:, 1:512], re, nsc[:, 0:511], ALU.mult)
    vector.tensor_tensor(dst[:, 1023:512:-1], imr, nsc[:, 0:511], ALU.mult)
    inst = vector.tensor_tensor(dst[:, 0:1024:512], sp, nsc[:, 1540:1542], ALU.mult)
    ve.inc(inst)


def build_l1a():
    nc = bass.Bass("TRN2", target_bir_lowering=False)
    i_x1 = nc.dram_tensor("X1", [C, TOWN], BF16, kind="ExternalInput").ap()
    i_wk = nc.dram_tensor("WK", [C, C], F32, kind="ExternalInput").ap()
    i_wv = nc.dram_tensor("WV", [C, C], F32, kind="ExternalInput").ap()
    i_bk = nc.dram_tensor("BKC", [128, NCH], F32, kind="ExternalInput").ap()
    i_bv = nc.dram_tensor("BVC", [128, NCH], F32, kind="ExternalInput").ap()
    i_cpk = nc.dram_tensor("CPK", [C, C], F32, kind="ExternalInput").ap()
    i_triu = nc.dram_tensor("TRIU", [128, 128], BF16, kind="ExternalInput").ap()
    o_xn1 = nc.dram_tensor("XN1", [C, TOWN], BF16, kind="ExternalOutput").ap()
    o_sloc = nc.dram_tensor("SLOC", [TOWN, C], F32, kind="ExternalOutput").ap()
    o_tot = nc.dram_tensor("TOT", [1, C], F32, kind="ExternalOutput").ap()

    ctx = contextlib.ExitStack()
    with ctx:
        sb = lambda *a: ctx.enter_context(nc.sbuf_tensor(*a))
        psm = lambda *a: ctx.enter_context(nc.psum_tensor(*a))
        x1_sb = sb("x1_sb", [128, NCH * TOWN], BF16)
        xn1_sb = sb("xn1_sb", [128, NCH * TOWN], BF16)
        cpk_sb = sb("cpk_sb", [128, NCH * C], BF16)
        fk_sb = sb("fk_sb", [128, NCH * C], BF16)
        fv_sb = sb("fv_sb", [128, NCH * C], BF16)
        wk_sb = sb("wk_sb", [128, NCH * C], BF16)
        wv_sb = sb("wv_sb", [128, NCH * C], BF16)
        bcol = sb("bcol", [128, 2 * NCH], BF16)
        brow = sb("brow", [1, 2 * C], BF16)
        triu = sb("triu", [128, 128], BF16)
        onescol = sb("onescol", [128, 1], BF16)
        onesrow = sb("onesrow", [1, 128], F32)
        onesrowb = sb("onesrowb", [1, 128], BF16)
        sqb = sb("sqb", [128, 2 * 512], BF16)
        rows = sb("rows", [1, 4 * 512], F32)
        kf = sb("kf", [128, 1024], BF16)
        vf = sb("vf", [128, 1024], BF16)
        kv = sb("kv", [128, 1024], BF16)
        nsc = sb("nsc", [128, 2560], F32)
        crep_sb = sb("crep_sb", [128, 1024], F32)
        carry = sb("carry", [1, 1024], F32)
        slst = sb("slst", [128, 2 * 1024], F32)
        gapb = sb("gapb", [128, 1], F32)
        pA = psm("pA", [128, 1024], F32)
        pB = psm("pB", [128, 1024], F32)
        pC = psm("pC", [128, 1024], F32)
        prow = psm("prow", [128, 1024], F32)
        s = _sems(nc, ctx)
        ld, mm, ve, ac, st = s["ld"], s["mm"], s["ve"], s["ac"], s["st"]

        def gap(vector):
            vector.tensor_copy(gapb[0:1, 0:1], onesrow[0:1, 0:1])

        with nc.Block() as block:
            @block.sync
            def _(sync):
                for c in range(NCH):
                    sync.dma_start(out=x1_sb[:, c * TOWN:(c + 1) * TOWN],
                                   in_=i_x1[c * 128:(c + 1) * 128, :]).then_inc(ld.sem, 16)
                    ld.v += 16
                sync.dma_start(out=triu[:, :], in_=i_triu).then_inc(ld.sem, 16)
                ld.v += 16

            @block.gpsimd
            def _(gpsimd):
                for c in range(NCH):
                    gpsimd.dma_start(out=cpk_sb[:, c * C:(c + 1) * C],
                                     in_=i_cpk[c * 128:(c + 1) * 128, :]).then_inc(ld.sem, 16)
                    ld.v += 16
                gpsimd.dma_start(out=bcol[:, 0:NCH], in_=i_bk).then_inc(ld.sem, 16)
                gpsimd.dma_start(out=bcol[:, NCH:2 * NCH], in_=i_bv).then_inc(ld.sem, 16)
                ld.v += 32

            ldv = ld.v

            @block.vector
            def _(vector):
                vector.wait_ge(ld.sem, ldv)
                vector.memset(onescol[:, :], 1.0)
                vector.memset(onesrow[:, :], 1.0)
                vector.memset(onesrowb[:, :], 1.0)
                vector.memset(carry[:, :], 0.0)
                inst = vector.memset(gapb[:, :], 0.0)
                ve.inc(inst)
        nc.all_engine_barrier()

        mmb, acb, veb = mm.v, ac.v, ve.v
        sched = {}
        m, a, v_ = mmb, acb, veb
        for h2 in range(2):
            tts = (2 * h2, 2 * h2 + 1)
            for cp in range(NCH):
                for tt in tts:
                    v_ += 1
                    sched[("sq", cp, tt)] = v_
                m += 1
                sched[("cols", h2, cp)] = m
            for tt in tts:
                v_ += 1
                sched[("rows1", tt)] = v_
                a += 1
                sched[("sqrt", tt)] = a
                v_ += 1
                sched[("rows2", tt)] = v_
                m += 1
                sched[("reps", tt)] = m
                for cp in range(NCH):
                    v_ += 1
                    sched[("app", tt, cp)] = v_

        with nc.Block() as block:
            @block.vector
            def _(vector):
                for h2 in range(2):
                    tts = (2 * h2, 2 * h2 + 1)
                    for cp in range(NCH):
                        for tt in tts:
                            if cp >= 1:
                                vector.wait_ge(mm.sem, sched[("cols", h2, cp - 1)])
                            xsl = x1_sb[:, cp * TOWN + tt * 512:cp * TOWN + (tt + 1) * 512]
                            inst = vector.tensor_tensor(
                                sqb[:, (tt % 2) * 512:(tt % 2 + 1) * 512], xsl, xsl, ALU.mult)
                            ve.inc(inst)
                            assert ve.v == sched[("sq", cp, tt)]
                    for tt in tts:
                        p32 = 32 * (tt % 2)
                        vector.wait_ge(mm.sem, sched[("cols", h2, NCH - 1)])
                        R = 512
                        vector.tensor_scalar_mul(rows[0:1, 0:R], prow[p32:p32 + 1, 0:512], 1.0 / C)
                        vector.tensor_scalar_mul(rows[0:1, R:2 * R],
                                                 prow[p32:p32 + 1, 512:1024], 1.0 / C)
                        vector.tensor_tensor(rows[0:1, 2 * R:3 * R], rows[0:1, 0:R],
                                             rows[0:1, 0:R], ALU.mult)
                        gap(vector)
                        vector.tensor_sub(rows[0:1, R:2 * R], rows[0:1, R:2 * R],
                                          rows[0:1, 2 * R:3 * R])
                        gap(vector)
                        inst = vector.tensor_scalar_add(rows[0:1, R:2 * R],
                                                        rows[0:1, R:2 * R], LN_EPS)
                        ve.inc(inst)
                        assert ve.v == sched[("rows1", tt)]
                        vector.wait_ge(ac.sem, sched[("sqrt", tt)])
                        vector.reciprocal(rows[0:1, 3 * R:4 * R], rows[0:1, 2 * R:3 * R])
                        gap(vector)
                        inst = vector.tensor_tensor(rows[0:1, R:2 * R], rows[0:1, 0:R],
                                                    rows[0:1, 3 * R:4 * R], ALU.mult)
                        ve.inc(inst)
                        assert ve.v == sched[("rows2", tt)]
                        for cp in range(NCH):
                            vector.wait_ge(mm.sem, sched[("reps", tt)])
                            xsl = x1_sb[:, cp * TOWN + tt * 512:cp * TOWN + (tt + 1) * 512]
                            nsl = xn1_sb[:, cp * TOWN + tt * 512:cp * TOWN + (tt + 1) * 512]
                            vector.tensor_tensor(slst[:, 0:512], xsl, pA[:, 0:512], ALU.mult)
                            gap(vector)
                            inst = vector.tensor_tensor(nsl, slst[:, 0:512],
                                                        pB[:, 0:512], ALU.subtract)
                            ve.inc(inst)
                            assert ve.v == sched[("app", tt, cp)]

            @block.tensor
            def _(tensor):
                for h2 in range(2):
                    tts = (2 * h2, 2 * h2 + 1)
                    for cp in range(NCH):
                        for tt in tts:
                            tensor.wait_ge(ve.sem, sched[("sq", cp, tt)])
                        if h2 == 1 and cp == 0:
                            tensor.wait_ge(ve.sem, sched[("app", 1, NCH - 1)])
                        for tt in tts:
                            p32 = 32 * (tt % 2)
                            xsl = x1_sb[:, cp * TOWN + tt * 512:cp * TOWN + (tt + 1) * 512]
                            inst = tensor.matmul(prow[p32:p32 + 1, 0:512], onescol[:, 0:1],
                                                 xsl, start=(cp == 0), stop=(cp == NCH - 1),
                                                 tile_position=(0, p32))
                            inst = tensor.matmul(prow[p32:p32 + 1, 512:1024], onescol[:, 0:1],
                                                 sqb[:, (tt % 2) * 512:(tt % 2 + 1) * 512],
                                                 start=(cp == 0), stop=(cp == NCH - 1),
                                                 tile_position=(0, p32))
                        mm.inc(inst)
                        assert mm.v == sched[("cols", h2, cp)]
                    for tt in tts:
                        tensor.wait_ge(ve.sem, sched[("rows2", tt)])
                        if tt > 0:
                            tensor.wait_ge(ve.sem, sched[("app", tt - 1, NCH - 1)])
                        tensor.matmul(pA[:, 0:512], onesrow[0:1, :],
                                      rows[0:1, 3 * 512:4 * 512], start=True, stop=True)
                        inst = tensor.matmul(pB[:, 0:512], onesrow[0:1, :],
                                             rows[0:1, 512:1024], start=True, stop=True)
                        mm.inc(inst)
                        assert mm.v == sched[("reps", tt)]

            @block.scalar
            def _(scalar):
                for h2 in range(2):
                    for tt in (2 * h2, 2 * h2 + 1):
                        scalar.wait_ge(ve.sem, sched[("rows1", tt)])
                        inst = scalar.activation(rows[0:1, 2 * 512:3 * 512],
                                                 rows[0:1, 512:1024], AF.Sqrt)
                        ac.inc(inst)
                        assert ac.v == sched[("sqrt", tt)]
        nc.all_engine_barrier()

        with nc.Block() as block:
            @block.gpsimd
            def _(gpsimd):
                for c in range(NCH):
                    gpsimd.dma_start(out=o_xn1[c * 128:(c + 1) * 128, :],
                                     in_=xn1_sb[:, c * TOWN:(c + 1) * TOWN]).then_inc(st.sem, 16)
                    st.v += 16
        nc.all_engine_barrier()

        mmb, acb, ldb = mm.v, ac.v, ld.v
        with nc.Block() as block:
            @block.gpsimd
            def _(gpsimd):
                for c in range(NCH):
                    gpsimd.dma_start(out=wk_sb[:, c * C:(c + 1) * C],
                                     in_=i_wk[c * 128:(c + 1) * 128, :]).then_inc(ld.sem, 16)
                    gpsimd.dma_start(out=wv_sb[:, c * C:(c + 1) * C],
                                     in_=i_wv[c * 128:(c + 1) * 128, :]).then_inc(ld.sem, 16)
                    ld.v += 32

            @block.tensor
            def _(tensor):
                tensor.wait_ge(ld.sem, ldb + NCH * 32)
                n = 0
                for sig, wsb in ((0, wk_sb), (1, wv_sb)):
                    for d in range(NCH):
                        b = n % 2
                        p = pA if b == 0 else pB
                        if n >= 2:
                            tensor.wait_ge(ac.sem, acb + n - 1)
                        for e in range(NCH):
                            for jj in range(2):
                                inst = tensor.matmul(
                                    p[:, jj * 512:(jj + 1) * 512],
                                    wsb[:, e * C + d * 128:e * C + (d + 1) * 128],
                                    cpk_sb[:, e * C + jj * 512:e * C + (jj + 1) * 512],
                                    start=(e == 0), stop=(e == NCH - 1))
                        mm.inc(inst)
                        n += 1
                    for e in range(NCH):
                        for jj in range(2):
                            inst = tensor.matmul(
                                prow[0:1, jj * 512:(jj + 1) * 512],
                                bcol[:, sig * NCH + e:sig * NCH + e + 1],
                                cpk_sb[:, e * C + jj * 512:e * C + (jj + 1) * 512],
                                start=(e == 0), stop=(e == NCH - 1))
                    mm.inc(inst)
                    n += 1

            @block.scalar
            def _(scalar):
                n = 0
                for sig in range(2):
                    fsb = fk_sb if sig == 0 else fv_sb
                    for d in range(NCH):
                        b = n % 2
                        p = pA if b == 0 else pB
                        scalar.wait_ge(mm.sem, mmb + n + 1)
                        inst = scalar.copy(fsb[:, d * C:(d + 1) * C], p[:, :])
                        ac.inc(inst)
                        n += 1
                    scalar.wait_ge(mm.sem, mmb + n + 1)
                    inst = scalar.copy(brow[0:1, sig * C:(sig + 1) * C], prow[0:1, 0:1024])
                    ac.inc(inst)
                    n += 1
        nc.all_engine_barrier()

        mmb, acb, veb, stb = mm.v, ac.v, ve.v, st.v
        MMC = 4
        ACC_ = 4
        VEC = 7
        with nc.Block() as block:
            @block.tensor
            def _(tensor):
                for t in range(16):
                    if t >= 1:
                        tensor.wait_ge(ve.sem, veb + VEC * (t - 1) + 6)
                    for c in range(NCH):
                        for jj in range(2):
                            inst = tensor.matmul(
                                pA[:, jj * 512:(jj + 1) * 512],
                                xn1_sb[:, c * TOWN + t * 128:c * TOWN + (t + 1) * 128],
                                fk_sb[:, c * C + jj * 512:c * C + (jj + 1) * 512],
                                start=(c == 0), stop=False)
                    for jj in range(2):
                        inst = tensor.matmul(pA[:, jj * 512:(jj + 1) * 512],
                                             onesrowb[0:1, :],
                                             brow[0:1, jj * 512:(jj + 1) * 512],
                                             start=False, stop=True)
                    mm.inc(inst)
                    if t >= 1:
                        tensor.wait_ge(ve.sem, veb + VEC * (t - 1) + 4)
                    for c in range(NCH):
                        for jj in range(2):
                            inst = tensor.matmul(
                                pB[:, jj * 512:(jj + 1) * 512],
                                xn1_sb[:, c * TOWN + t * 128:c * TOWN + (t + 1) * 128],
                                fv_sb[:, c * C + jj * 512:c * C + (jj + 1) * 512],
                                start=(c == 0), stop=False)
                    for jj in range(2):
                        inst = tensor.matmul(pB[:, jj * 512:(jj + 1) * 512],
                                             onesrowb[0:1, :],
                                             brow[0:1, C + jj * 512:C + (jj + 1) * 512],
                                             start=False, stop=True)
                    mm.inc(inst)
                    tensor.wait_ge(ve.sem, veb + VEC * t + 5)
                    if t >= 1:
                        tensor.wait_ge(ve.sem, veb + VEC * (t - 1) + 7)
                    for jj in range(2):
                        inst = tensor.matmul(pC[:, jj * 512:(jj + 1) * 512],
                                             triu[:, :], kv[:, jj * 512:(jj + 1) * 512],
                                             start=True, stop=True)
                    mm.inc(inst)
                    if t >= 1:
                        tensor.wait_ge(ve.sem, veb + VEC * (t - 1) + 7)
                    for jj in range(2):
                        inst = tensor.matmul(prow[0:1, jj * 512:(jj + 1) * 512],
                                             onescol[:, 0:1], kv[:, jj * 512:(jj + 1) * 512],
                                             start=True, stop=True)
                    if t >= 1:
                        tensor.wait_ge(ac.sem, acb + 5 * (t - 1) + 5)
                        tensor.wait_ge(ve.sem, veb + VEC * (t - 1) + 7)
                    tensor.wait_ge(ve.sem, veb + VEC * t + 2)
                    for jj in range(2):
                        inst = tensor.matmul(pA[:, jj * 512:(jj + 1) * 512],
                                             onesrow[0:1, :], carry[0:1, jj * 512:(jj + 1) * 512],
                                             start=True, stop=True)
                    mm.inc(inst)

            @block.scalar
            def _(scalar):
                for t in range(16):
                    scalar.wait_ge(mm.sem, mmb + MMC * t + 1)
                    if t >= 1:
                        scalar.wait_ge(ve.sem, veb + VEC * (t - 1) + 5)
                    _unorm_act0(scalar, pA, nsc, ac)
                    scalar.wait_ge(ve.sem, veb + VEC * t + 1)
                    _unorm_act1(scalar, nsc, ac)
                    scalar.wait_ge(mm.sem, mmb + MMC * t + 2)
                    scalar.wait_ge(ve.sem, veb + VEC * t + 2)
                    _unorm_act0(scalar, pB, nsc, ac)
                    scalar.wait_ge(ve.sem, veb + VEC * t + 3)
                    _unorm_act1(scalar, nsc, ac)
                    scalar.wait_ge(mm.sem, mmb + MMC * t + 4)
                    inst = scalar.copy(crep_sb[:, :], pA[:, :])
                    ac.inc(inst)

            @block.vector
            def _(vector):
                for t in range(16):
                    vector.wait_ge(ac.sem, acb + 5 * t + 1)
                    _unorm_dve1(vector, nsc, ve)
                    vector.wait_ge(ac.sem, acb + 5 * t + 2)
                    _unorm_post(vector, pA, kf, nsc, ve)
                    vector.wait_ge(ac.sem, acb + 5 * t + 3)
                    _unorm_dve1(vector, nsc, ve)
                    vector.wait_ge(ac.sem, acb + 5 * t + 4)
                    _unorm_post(vector, pB, vf, nsc, ve)
                    vector.tensor_tensor(nsc[:, 0:511], kf[:, 1:512], vf[:, 1:512], ALU.mult)
                    vector.tensor_tensor(nsc[:, 512:1023], kf[:, 1023:512:-1],
                                         vf[:, 1023:512:-1], ALU.mult)
                    vector.tensor_tensor(nsc[:, 1024:1026], kf[:, 0:1024:512],
                                         vf[:, 0:1024:512], ALU.mult)
                    vector.tensor_sub(kv[:, 1:512], nsc[:, 0:511], nsc[:, 512:1023])
                    vector.tensor_tensor(nsc[:, 0:511], kf[:, 1:512],
                                         vf[:, 1023:512:-1], ALU.mult)
                    vector.tensor_tensor(nsc[:, 512:1023], kf[:, 1023:512:-1],
                                         vf[:, 1:512], ALU.mult)
                    vector.tensor_copy(kv[:, 0:1024:512], nsc[:, 1024:1026])
                    inst = vector.tensor_tensor(kv[:, 1023:512:-1], nsc[:, 0:511],
                                                nsc[:, 512:1023], ALU.add)
                    ve.inc(inst)
                    b = t % 2
                    vector.wait_ge(mm.sem, mmb + MMC * t + 4)
                    vector.wait_ge(ac.sem, acb + 5 * t + 5)
                    if t >= 2:
                        vector.wait_ge(st.sem, stb + 16 * (t - 1))
                    inst = vector.tensor_tensor(slst[:, b * 1024:(b + 1) * 1024],
                                                pC[:, :], crep_sb[:, :], ALU.add)
                    ve.inc(inst)
                    inst = vector.tensor_tensor(carry[0:1, :], carry[0:1, :],
                                                prow[0:1, 0:1024], ALU.add)
                    ve.inc(inst)

            @block.gpsimd
            def _(gpsimd):
                for t in range(16):
                    b = t % 2
                    gpsimd.wait_ge(ve.sem, veb + VEC * t + 6)
                    gpsimd.dma_start(out=o_sloc[t * 128:(t + 1) * 128, :],
                                     in_=slst[:, b * 1024:(b + 1) * 1024]).then_inc(st.sem, 16)
                    st.v += 16
                gpsimd.wait_ge(ve.sem, veb + VEC * 15 + 7)
                gpsimd.dma_start(out=o_tot, in_=carry[0:1, :]).then_inc(st.sem, 16)
                st.v += 16
                gpsimd.wait_ge(st.sem, st.v)
    return nc




def build_l1b():
    nc = bass.Bass("TRN2", target_bir_lowering=False)
    i_xn1 = nc.dram_tensor("XN1", [C, TOWN], BF16, kind="ExternalInput").ap()
    i_wq = nc.dram_tensor("WQ", [C, C], F32, kind="ExternalInput").ap()
    i_bq = nc.dram_tensor("BQC", [128, NCH], F32, kind="ExternalInput").ap()
    i_cpk = nc.dram_tensor("CPK", [C, C], F32, kind="ExternalInput").ap()
    i_cinv = nc.dram_tensor("CINV", [C, C], F32, kind="ExternalInput").ap()
    i_sloc = nc.dram_tensor("SLOC", [TOWN, C], F32, kind="ExternalInput").ap()
    i_totp = nc.dram_tensor("TOTP", [1, C], F32, kind="ExternalInput").ap()
    i_ident = nc.dram_tensor("IDENT", [128, 128], BF16, kind="ExternalInput").ap()
    o_att1 = nc.dram_tensor("ATT1", [C, TOWN], BF16, kind="ExternalOutput").ap()

    ctx = contextlib.ExitStack()
    with ctx:
        sb = lambda *a: ctx.enter_context(nc.sbuf_tensor(*a))
        psm = lambda *a: ctx.enter_context(nc.psum_tensor(*a))
        xn1_sb = sb("xn1_sb", [128, NCH * TOWN], BF16)
        cslot = sb("cslot", [128, NCH * C], BF16)
        wq_sb = sb("wq_sb", [128, NCH * C], BF16)
        fq_sb = sb("fq_sb", [128, NCH * C], BF16)
        yfm = sb("yfm", [128, NCH * TOWN], BF16)
        brow = sb("brow", [1, C], BF16)
        bcol = sb("bcol", [128, NCH], BF16)
        totp = sb("totp", [1, C], F32)
        totr = sb("totr", [128, 1024], F32)
        ssb = sb("ssb", [128, 2 * 1024], F32)
        qf = sb("qf", [128, 1024], BF16)
        ybuf = sb("ybuf", [128, 1024], BF16)
        nsc = sb("nsc", [128, 2560], F32)
        atst = sb("atst", [128, 2 * 512], BF16)
        ident = sb("ident", [128, 128], BF16)
        onesrow = sb("onesrow", [1, 128], F32)
        onesrowb = sb("onesrowb", [1, 128], BF16)
        gapb = sb("gapb", [1, 1], F32)
        pA = psm("pA", [128, 1024], F32)
        pT = psm("pT", [128, 1024], BF16)
        pC = psm("pC", [128, 1024], F32)
        pD = [psm("pD0", [128, 512], F32), psm("pD1", [128, 512], F32)]
        s = _sems(nc, ctx)
        ld, mm, ve, ac, st = s["ld"], s["mm"], s["ve"], s["ac"], s["st"]

        with nc.Block() as block:
            @block.sync
            def _(sync):
                for c in range(NCH):
                    sync.dma_start(out=xn1_sb[:, c * TOWN:(c + 1) * TOWN],
                                   in_=i_xn1[c * 128:(c + 1) * 128, :]).then_inc(ld.sem, 16)
                    ld.v += 16
                sync.dma_start(out=totp[:, :], in_=i_totp).then_inc(ld.sem, 16)
                sync.dma_start(out=ident[:, :], in_=i_ident).then_inc(ld.sem, 16)
                ld.v += 32

            @block.gpsimd
            def _(gpsimd):
                for c in range(NCH):
                    gpsimd.dma_start(out=cslot[:, c * C:(c + 1) * C],
                                     in_=i_cpk[c * 128:(c + 1) * 128, :]).then_inc(ld.sem, 16)
                    gpsimd.dma_start(out=wq_sb[:, c * C:(c + 1) * C],
                                     in_=i_wq[c * 128:(c + 1) * 128, :]).then_inc(ld.sem, 16)
                    ld.v += 32
                gpsimd.dma_start(out=bcol[:, :], in_=i_bq).then_inc(ld.sem, 16)
                ld.v += 16

            ldv = ld.v

            @block.vector
            def _(vector):
                vector.wait_ge(ld.sem, ldv)
                vector.memset(onesrow[:, :], 1.0)
                inst = vector.memset(onesrowb[:, :], 1.0)
                ve.inc(inst)
        nc.all_engine_barrier()

        mmb, acb = mm.v, ac.v
        with nc.Block() as block:
            @block.tensor
            def _(tensor):
                n = 0
                for d in range(NCH):
                    b = n % 2
                    if n >= 2:
                        tensor.wait_ge(ac.sem, acb + n - 1)
                    for e in range(NCH):
                        for jj in range(2):
                            inst = tensor.matmul(
                                pA[:, jj * 512:(jj + 1) * 512] if b == 0 else
                                pC[:, jj * 512:(jj + 1) * 512],
                                wq_sb[:, e * C + d * 128:e * C + (d + 1) * 128],
                                cslot[:, e * C + jj * 512:e * C + (jj + 1) * 512],
                                start=(e == 0), stop=(e == NCH - 1))
                    mm.inc(inst)
                    n += 1
                tensor.wait_ge(ac.sem, acb + NCH - 1)
                for e in range(NCH):
                    for jj in range(2):
                        inst = tensor.matmul(
                            pA[0:1, jj * 512:(jj + 1) * 512],
                            bcol[:, e:e + 1],
                            cslot[:, e * C + jj * 512:e * C + (jj + 1) * 512],
                            start=(e == 0), stop=(e == NCH - 1))
                mm.inc(inst)
                n += 1
                tensor.wait_ge(ac.sem, acb + NCH)
                for jj in range(2):
                    inst = tensor.matmul(pC[:, jj * 512:(jj + 1) * 512],
                                         onesrow[0:1, :], totp[0:1, jj * 512:(jj + 1) * 512],
                                         start=True, stop=True)
                mm.inc(inst)

            @block.scalar
            def _(scalar):
                n = 0
                for d in range(NCH):
                    b = n % 2
                    scalar.wait_ge(mm.sem, mmb + n + 1)
                    p = pA if b == 0 else pC
                    inst = scalar.copy(fq_sb[:, d * C:(d + 1) * C], p[:, :])
                    ac.inc(inst)
                    n += 1
                scalar.wait_ge(mm.sem, mmb + n + 1)
                inst = scalar.copy(brow[0:1, :], pA[0:1, 0:1024])
                ac.inc(inst)
                scalar.wait_ge(mm.sem, mmb + n + 2)
                inst = scalar.copy(totr[:, :], pC[:, :])
                ac.inc(inst)

            @block.gpsimd
            def _(gpsimd):
                gpsimd.wait_ge(ac.sem, acb + NCH + 2)
                for c in range(NCH):
                    gpsimd.dma_start(out=cslot[:, c * C:(c + 1) * C],
                                     in_=i_cinv[c * 128:(c + 1) * 128, :]).then_inc(ld.sem, 16)
                    ld.v += 16
        nc.all_engine_barrier()

        mmb, acb, veb, ldb = mm.v, ac.v, ve.v, ld.v
        MMC = 3
        ACC_ = 4
        with nc.Block() as block:
            @block.sync
            def _(sync):
                for t in range(16):
                    b = t % 2
                    if t >= 2:
                        sync.wait_ge(ve.sem, veb + 5 * (t - 2) + 5)
                    sync.dma_start(out=ssb[:, b * 1024:(b + 1) * 1024],
                                   in_=i_sloc[t * 128:(t + 1) * 128, :]).then_inc(ld.sem, 16)
                    ld.v += 16

            @block.tensor
            def _(tensor):
                for t in range(16):
                    if t >= 1:
                        tensor.wait_ge(ve.sem, veb + 5 * (t - 1) + 2)
                    for c in range(NCH):
                        for jj in range(2):
                            inst = tensor.matmul(
                                pA[:, jj * 512:(jj + 1) * 512],
                                xn1_sb[:, c * TOWN + t * 128:c * TOWN + (t + 1) * 128],
                                fq_sb[:, c * C + jj * 512:c * C + (jj + 1) * 512],
                                start=(c == 0), stop=False)
                    for jj in range(2):
                        inst = tensor.matmul(pA[:, jj * 512:(jj + 1) * 512],
                                             onesrowb[0:1, :],
                                             brow[0:1, jj * 512:(jj + 1) * 512],
                                             start=False, stop=True)
                    mm.inc(inst)
                    tensor.wait_ge(ve.sem, veb + 5 * t + 5)
                    if t >= 1:
                        tensor.wait_ge(ac.sem, acb + 3 * (t - 1) + 3)
                    for fc in range(NCH):
                        inst = tensor.transpose(pT[:, fc * 128:(fc + 1) * 128],
                                                ybuf[:, fc * 128:(fc + 1) * 128],
                                                ident[:, :])
                    mm.inc(inst)

            @block.scalar
            def _(scalar):
                for t in range(16):
                    scalar.wait_ge(mm.sem, mmb + 2 * t + 1)
                    if t >= 1:
                        scalar.wait_ge(ve.sem, veb + 5 * (t - 1) + 5)
                    _unorm_act0(scalar, pA, nsc, ac)
                    scalar.wait_ge(ve.sem, veb + 5 * t + 1)
                    _unorm_act1(scalar, nsc, ac)
                    scalar.wait_ge(mm.sem, mmb + 2 * t + 2)
                    dst = _sbap(yfm, t * 128, [[NCH * TOWN, 128], [TOWN, NCH], [1, 128]])
                    src2 = _sbap(pT, 0, [[1024, 128], [128, NCH], [1, 128]])
                    inst = scalar.copy(dst, src2)
                    ac.inc(inst)

            @block.vector
            def _(vector):
                for t in range(16):
                    b = t % 2
                    vector.wait_ge(ac.sem, acb + 3 * t + 1)
                    _unorm_dve1(vector, nsc, ve)
                    vector.wait_ge(ac.sem, acb + 3 * t + 2)
                    _unorm_post(vector, pA, qf, nsc, ve)
                    vector.wait_ge(ld.sem, ldb + 16 * (t + 1))
                    inst = vector.tensor_tensor(ssb[:, b * 1024:(b + 1) * 1024],
                                                ssb[:, b * 1024:(b + 1) * 1024],
                                                totr[:, :], ALU.add)
                    ve.inc(inst)
                    S = ssb[:, b * 1024:(b + 1) * 1024]
                    sre = S[:, 1:512]
                    sim = S[:, 1023:512:-1]
                    ssp = S[:, 0:1024:512]
                    qre = qf[:, 1:512]
                    qim = qf[:, 1023:512:-1]
                    qsp = qf[:, 0:1024:512]
                    vector.tensor_tensor(nsc[:, 0:511], sre, qre, ALU.mult)
                    vector.tensor_tensor(nsc[:, 512:1023], sim, qim, ALU.mult)
                    vector.tensor_tensor(nsc[:, 1024:1026], ssp, qsp, ALU.mult)
                    inst = vector.tensor_add(ybuf[:, 1:512], nsc[:, 0:511],
                                             nsc[:, 512:1023])
                    ve.inc(inst)
                    vector.tensor_tensor(nsc[:, 0:511], sim, qre, ALU.mult)
                    vector.tensor_tensor(nsc[:, 512:1023], sre, qim, ALU.mult)
                    vector.tensor_copy(ybuf[:, 0:1024:512], nsc[:, 1024:1026])
                    inst = vector.tensor_sub(ybuf[:, 1023:512:-1], nsc[:, 0:511],
                                             nsc[:, 512:1023])
                    ve.inc(inst)
        nc.all_engine_barrier()

        mmb, acb, stb, ldb2 = mm.v, ac.v, st.v, ld.v
        with nc.Block() as block:
            @block.tensor
            def _(tensor):
                n = 0
                for cp in range(NCH):
                    for tt in range(4):
                        b = n % 2
                        if n == 0:
                            tensor.wait_ge(ld.sem, ldb2)
                        if n >= 2:
                            tensor.wait_ge(ac.sem, acb + n - 1)
                        for fc in range(NCH):
                            inst = tensor.matmul(
                                pD[b][:, :],
                                cslot[:, fc * C + cp * 128:fc * C + (cp + 1) * 128],
                                yfm[:, fc * TOWN + tt * 512:fc * TOWN + (tt + 1) * 512],
                                start=(fc == 0), stop=(fc == NCH - 1))
                        mm.inc(inst)
                        n += 1

            @block.scalar
            def _(scalar):
                n = 0
                for cp in range(NCH):
                    for tt in range(4):
                        b = n % 2
                        scalar.wait_ge(mm.sem, mmb + n + 1)
                        if n >= 2:
                            scalar.wait_ge(st.sem, stb + 16 * (n - 1))
                        inst = scalar.copy(atst[:, b * 512:(b + 1) * 512], pD[b][:, :])
                        ac.inc(inst)
                        n += 1

            @block.gpsimd
            def _(gpsimd):
                n = 0
                for cp in range(NCH):
                    for tt in range(4):
                        b = n % 2
                        gpsimd.wait_ge(ac.sem, acb + n + 1)
                        gpsimd.dma_start(out=o_att1[cp * 128:(cp + 1) * 128,
                                                    tt * 512:(tt + 1) * 512],
                                         in_=atst[:, b * 512:(b + 1) * 512]).then_inc(st.sem, 16)
                        st.v += 16
                        n += 1
                gpsimd.wait_ge(st.sem, st.v)
    return nc


def build_l1c():
    nc = bass.Bass("TRN2", target_bir_lowering=False)
    i_x1 = nc.dram_tensor("X1", [C, TOWN], BF16, kind="ExternalInput").ap()
    i_xn1 = nc.dram_tensor("XN1", [C, TOWN], BF16, kind="ExternalInput").ap()
    i_bnd = nc.dram_tensor("BND", [128, NCH], BF16, kind="ExternalInput").ap()
    i_att1 = nc.dram_tensor("ATT1", [C, TOWN], BF16, kind="ExternalInput").ap()
    i_mur = nc.dram_tensor("MURC", [128, NCH], F32, kind="ExternalInput").ap()
    i_muk = nc.dram_tensor("MUKC", [128, NCH], F32, kind="ExternalInput").ap()
    i_wrt = nc.dram_tensor("WRT", [C, C], F32, kind="ExternalInput").ap()
    i_wkt = nc.dram_tensor("WKT", [C, C], F32, kind="ExternalInput").ap()
    i_wvt = nc.dram_tensor("WVT2", [C, C], F32, kind="ExternalInput").ap()
    o_x = nc.dram_tensor("XOUT", [C, TOWN], F32, kind="ExternalOutput").ap()

    ctx = contextlib.ExitStack()
    with ctx:
        sb = lambda *a: ctx.enter_context(nc.sbuf_tensor(*a))
        psm = lambda *a: ctx.enter_context(nc.psum_tensor(*a))
        x1_sb = sb("x1_sb", [128, NCH * TOWN], BF16)
        xn1_sb = sb("xn1_sb", [128, NCH * TOWN], BF16)
        kact = xn1_sb
        xr = sb("xr", [128, NCH * TOWN], BF16)
        xk = sb("xk", [128, NCH * TOWN], BF16)
        slotA = sb("slotA", [128, NCH * C], BF16)
        slotB = sb("slotB", [128, NCH * C], BF16)
        bnd = sb("bnd", [128, NCH], BF16)
        murc = sb("murc", [128, NCH], F32)
        mukc = sb("mukc", [128, NCH], F32)
        dsc = sb("dsc", [128, 2 * 512], F32)
        krs = sb("krs", [128, 2 * 512], BF16)
        asb = sb("asb", [128, 2 * 512], BF16)
        xo = sb("xo", [128, 2 * 512], F32)
        ts1 = sb("ts1", [128, 2 * 512], F32)
        gapb = sb("gapb", [1, 1], F32)
        pA = [psm("pA0", [128, 512], F32), psm("pA1", [128, 512], F32)]
        pB = [psm("pB0", [128, 512], F32), psm("pB1", [128, 512], F32)]
        s = _sems(nc, ctx)
        ld, mm, ve, ac, st = s["ld"], s["mm"], s["ve"], s["ac"], s["st"]

        with nc.Block() as block:
            @block.sync
            def _(sync):
                for c in range(NCH):
                    sync.dma_start(out=x1_sb[:, c * TOWN:(c + 1) * TOWN],
                                   in_=i_x1[c * 128:(c + 1) * 128, :]).then_inc(ld.sem, 16)
                    sync.dma_start(out=xn1_sb[:, c * TOWN:(c + 1) * TOWN],
                                   in_=i_xn1[c * 128:(c + 1) * 128, :]).then_inc(ld.sem, 16)
                    ld.v += 32
                sync.dma_start(out=bnd[:, :], in_=i_bnd).then_inc(ld.sem, 16)
                sync.dma_start(out=murc[:, :], in_=i_mur).then_inc(ld.sem, 16)
                sync.dma_start(out=mukc[:, :], in_=i_muk).then_inc(ld.sem, 16)
                ld.v += 48

            @block.gpsimd
            def _(gpsimd):
                for c in range(NCH):
                    gpsimd.dma_start(out=slotA[:, c * C:(c + 1) * C],
                                     in_=i_wkt[c * 128:(c + 1) * 128, :]).then_inc(ld.sem, 16)
                    ld.v += 16

            ldv = ld.v

            @block.vector
            def _(vector):
                vector.wait_ge(ld.sem, ldv)
                inst = vector.memset(gapb[:, :], 0.0)
                ve.inc(inst)
        nc.all_engine_barrier()

        veb = ve.v
        with nc.Block() as block:
            @block.vector
            def _(vector):
                for c in range(NCH):
                    for part, dst, mu in ((0, xr, murc), (1, xk, mukc)):
                        cur0 = xn1_sb[:, c * TOWN:c * TOWN + 1]
                        vector.tensor_tensor(dsc[:, 0:1], cur0, bnd[:, c:c + 1],
                                             ALU.subtract)
                        vector.tensor_copy(gapb[0:1, 0:1], gapb[0:1, 0:1])
                        vector.scalar_tensor_tensor(dst[:, c * TOWN:c * TOWN + 1],
                                                    dsc[:, 0:1], mu[:, c:c + 1],
                                                    bnd[:, c:c + 1], ALU.mult, ALU.add)
                        cur = xn1_sb[:, c * TOWN + 1:c * TOWN + TOWN]
                        prev = xn1_sb[:, c * TOWN:c * TOWN + TOWN - 1]
                        vector.tensor_tensor(dsc[:, 0:512], cur[:, 0:512],
                                             prev[:, 0:512], ALU.subtract)
                        vector.tensor_copy(gapb[0:1, 0:1], gapb[0:1, 0:1])
                        for q4 in range(4):
                            a0 = q4 * 512
                            w = 512 if q4 < 3 else 511
                            if q4 > 0:
                                vector.tensor_tensor(dsc[:, 0:w], cur[:, a0:a0 + w],
                                                     prev[:, a0:a0 + w], ALU.subtract)
                                vector.tensor_copy(gapb[0:1, 0:1], gapb[0:1, 0:1])
                            inst = vector.scalar_tensor_tensor(
                                dst[:, c * TOWN + 1 + a0:c * TOWN + 1 + a0 + w],
                                dsc[:, 0:w], mu[:, c:c + 1], prev[:, a0:a0 + w],
                                ALU.mult, ALU.add)
                        ve.inc(inst)
        nc.all_engine_barrier()

        mmb, acb, veb, ldb = mm.v, ac.v, ve.v, ld.v
        with nc.Block() as block:
            @block.tensor
            def _(tensor):
                n = 0
                for cp in range(NCH):
                    for tt in range(4):
                        b = n % 2
                        if n >= 2:
                            tensor.wait_ge(ac.sem, acb + n - 1)
                        for c in range(NCH):
                            inst = tensor.matmul(
                                pA[b][:, :],
                                slotA[:, c * C + cp * 128:c * C + (cp + 1) * 128],
                                xk[:, c * TOWN + tt * 512:c * TOWN + (tt + 1) * 512],
                                start=(c == 0), stop=(c == NCH - 1))
                        mm.inc(inst)
                        n += 1

            @block.scalar
            def _(scalar):
                n = 0
                for cp in range(NCH):
                    for tt in range(4):
                        b = n % 2
                        scalar.wait_ge(mm.sem, mmb + n + 1)
                        if n >= 2:
                            scalar.wait_ge(ve.sem, veb + n - 1)
                        inst = scalar.activation(krs[:, b * 512:(b + 1) * 512],
                                                 pA[b][:, :], AF.Relu)
                        ac.inc(inst)
                        n += 1

            @block.vector
            def _(vector):
                n = 0
                for cp in range(NCH):
                    for tt in range(4):
                        b = n % 2
                        vector.wait_ge(ac.sem, acb + n + 1)
                        ks = krs[:, b * 512:(b + 1) * 512]
                        inst = vector.tensor_tensor(
                            kact[:, cp * TOWN + tt * 512:cp * TOWN + (tt + 1) * 512],
                            ks, ks, ALU.mult)
                        ve.inc(inst)
                        n += 1

            @block.gpsimd
            def _(gpsimd):
                gpsimd.wait_ge(mm.sem, mmb + 32)
                for c in range(NCH):
                    gpsimd.dma_start(out=slotA[:, c * C:(c + 1) * C],
                                     in_=i_wrt[c * 128:(c + 1) * 128, :]).then_inc(ld.sem, 16)
                    gpsimd.dma_start(out=slotB[:, c * C:(c + 1) * C],
                                     in_=i_wvt[c * 128:(c + 1) * 128, :]).then_inc(ld.sem, 16)
                    ld.v += 32
        nc.all_engine_barrier()

        mmb, acb, veb, stb, ldb = mm.v, ac.v, ve.v, st.v, ld.v
        with nc.Block() as block:
            @block.sync
            def _(sync):
                n = 0
                for cp in range(NCH):
                    for tt in range(4):
                        b = n % 2
                        if n >= 2:
                            sync.wait_ge(ve.sem, veb + 3 * (n - 2) + 3)
                        sync.dma_start(out=asb[:, b * 512:(b + 1) * 512],
                                       in_=i_att1[cp * 128:(cp + 1) * 128,
                                                  tt * 512:(tt + 1) * 512]).then_inc(ld.sem, 16)
                        n += 1
            ld.v += 32 * 16

            @block.tensor
            def _(tensor):
                n = 0
                for cp in range(NCH):
                    for tt in range(4):
                        b = n % 2
                        if n == 0:
                            tensor.wait_ge(ld.sem, ldb)
                        if n >= 2:
                            tensor.wait_ge(ac.sem, acb + n - 1)
                            tensor.wait_ge(ve.sem, veb + 3 * (n - 2) + 2)
                        for c in range(NCH):
                            inst = tensor.matmul(
                                pA[b][:, :],
                                slotA[:, c * C + cp * 128:c * C + (cp + 1) * 128],
                                xr[:, c * TOWN + tt * 512:c * TOWN + (tt + 1) * 512],
                                start=(c == 0), stop=(c == NCH - 1))
                        mm.inc(inst)
                        for c in range(NCH):
                            inst = tensor.matmul(
                                pB[b][:, :],
                                slotB[:, c * C + cp * 128:c * C + (cp + 1) * 128],
                                kact[:, c * TOWN + tt * 512:c * TOWN + (tt + 1) * 512],
                                start=(c == 0), stop=(c == NCH - 1))
                        mm.inc(inst)
                        n += 1

            @block.scalar
            def _(scalar):
                n = 0
                for cp in range(NCH):
                    for tt in range(4):
                        b = n % 2
                        scalar.wait_ge(mm.sem, mmb + 2 * n + 1)
                        inst = scalar.activation(krs[:, b * 512:(b + 1) * 512],
                                                 pA[b][:, :], AF.Sigmoid)
                        ac.inc(inst)
                        n += 1

            @block.vector
            def _(vector):
                n = 0
                for cp in range(NCH):
                    for tt in range(4):
                        b = n % 2
                        vector.wait_ge(ac.sem, acb + n + 1)
                        vector.wait_ge(mm.sem, mmb + 2 * n + 2)
                        vector.wait_ge(ld.sem, ldb + 16 * (n + 1))
                        inst = vector.tensor_tensor(ts1[:, b * 512:(b + 1) * 512],
                                                    krs[:, b * 512:(b + 1) * 512],
                                                    pB[b][:, :], ALU.mult)
                        ve.inc(inst)
                        vector.tensor_copy(gapb[0:1, 0:1], gapb[0:1, 0:1])
                        inst = vector.tensor_tensor(ts1[:, b * 512:(b + 1) * 512],
                                                    ts1[:, b * 512:(b + 1) * 512],
                                                    asb[:, b * 512:(b + 1) * 512], ALU.add)
                        ve.inc(inst)
                        if n >= 2:
                            vector.wait_ge(st.sem, stb + 16 * (n - 1))
                        vector.tensor_copy(gapb[0:1, 0:1], gapb[0:1, 0:1])
                        inst = vector.tensor_tensor(
                            xo[:, b * 512:(b + 1) * 512],
                            ts1[:, b * 512:(b + 1) * 512],
                            x1_sb[:, cp * TOWN + tt * 512:cp * TOWN + (tt + 1) * 512],
                            ALU.add)
                        ve.inc(inst)
                        n += 1

            @block.gpsimd
            def _(gpsimd):
                n = 0
                for cp in range(NCH):
                    for tt in range(4):
                        b = n % 2
                        gpsimd.wait_ge(ve.sem, veb + 3 * n + 3)
                        gpsimd.dma_start(out=o_x[cp * 128:(cp + 1) * 128,
                                                 tt * 512:(tt + 1) * 512],
                                         in_=xo[:, b * 512:(b + 1) * 512]).then_inc(st.sem, 16)
                        st.v += 16
                        n += 1
                gpsimd.wait_ge(st.sem, st.v)
    return nc


def _interleave_perm():
    p = np.empty(64, dtype=np.int64)
    p[0::2] = np.arange(32)
    p[1::2] = np.arange(32) + 32
    return p


_PERM64 = _interleave_perm()
_INDCOL = np.zeros((128, 2), np.float32)
_INDCOL[0:64, 0] = 1.0
_INDCOL[64:128, 1] = 1.0
_INDROW = np.zeros((2, 128), np.float32)
_INDROW[0, 0:64] = 1.0
_INDROW[1, 64:128] = 1.0
_EPS2 = np.full((2, 1), RMS_EPS, np.float32)


def _rope_tables(positions):
    T = len(positions)
    inv_freq = 1.0 / (10000.0 ** (np.arange(32, dtype=np.float64) * 2.0 / 64.0))
    ang = positions[:, None].astype(np.float64) * inv_freq[None, :]
    cos = np.cos(ang).T.astype(np.float32)
    sin = np.sin(ang).T.astype(np.float32)
    cos128 = np.empty((128, T), dtype=np.float32)
    sin128 = np.empty((128, T), dtype=np.float32)
    for base in (0, 64):
        cos128[base + 0:base + 64:2] = cos
        cos128[base + 1:base + 64:2] = cos
        sin128[base + 0:base + 64:2] = -sin
        sin128[base + 1:base + 64:2] = sin
    return cos128, sin128


def _prep_l0a(inputs, b, h):
    x = np.asarray(inputs["L0_attn_x"][b], np.float32)
    if h == 1:
        xkv = x
    else:
        xkv = np.concatenate([np.zeros((2048, C), np.float32), x[:2048]], axis=0)
    own_pos = np.arange(2048) + (2048 if h == 1 else 0)
    kv_pos = np.concatenate([np.arange(2048), own_pos])
    cosq, sinq = _rope_tables(own_pos)
    cosk, sink = _rope_tables(kv_pos)
    wq = np.asarray(inputs["L0_attn_w_qkv"], np.float32)
    perm_q = np.concatenate([_PERM64 + 64 * hh for hh in range(H)])
    wqk = np.empty((C, 2 * C), np.float32)
    wqk[:, 0:C] = wq[perm_q, :].T
    wqk[:, C:2 * C] = wq[1024 + perm_q, :].T
    qnp = np.asarray(inputs["L0_attn_q_norm_w"], np.float32)[_PERM64]
    knp = np.asarray(inputs["L0_attn_k_norm_w"], np.float32)[_PERM64]
    return {
        "XKV": np.ascontiguousarray(xkv),
        "WQKT": wqk,
        "WVT": np.ascontiguousarray(wq[2048:3072, :].T),
        "COSQ": cosq, "SINQ": sinq, "COSK": cosk, "SINK": sink,
        "QNP": np.tile(qnp, 2).reshape(128, 1).astype(np.float32),
        "KNP": np.tile(knp, 2).reshape(128, 1).astype(np.float32),
        "IDENT": np.eye(128, dtype=np.float32).astype(BF),
        "INDCOL": _INDCOL, "INDROW": _INDROW, "EPS2": _EPS2,
    }


def _maskc_host():
    i = np.arange(128)[:, None]
    c = np.arange(2176)[None, :]
    return ((c - 1024) >= i).astype(np.float32).astype(BF)


_MASKC = _maskc_host()


def _prep_l0b(inputs, b, h, l0a_out):
    r = l0a_out
    rkrows = np.asarray(r["RKROWS"])
    rk = np.empty((128, 512), np.float32)
    kt = np.arange(32)
    for hh in range(H):
        rk[:, hh * 32 + kt] = rkrows[hh].reshape(32, 128).T
    small = np.zeros((1, 129), np.float32)
    small[0, 0:64] = np.asarray(inputs["L0_attn_q_norm_w"], np.float32)
    small[0, 64:128] = np.asarray(inputs["L0_attn_k_norm_w"], np.float32)
    small[0, 128] = 0.0 if h == 1 else NEG
    return {
        "Q": np.asarray(r["Q"]), "K": np.asarray(r["K"]), "V": np.asarray(r["V"]),
        "RK": rk, "MASKC": _MASKC, "SMALL": small,
    }


def _prm_t(a):
    return np.ascontiguousarray(
        np.asarray(a, np.float32).reshape(4, 8, 128).transpose(0, 2, 1).reshape(512, 8))


def _prep_l0c(inputs, b, h, ra, rb):
    mrc = np.asarray(ra["MRC"], np.float32)
    mr2 = np.empty((2, TOWN), np.float32)
    for t in range(16, 32):
        mr2[0, (t - 16) * 128:(t - 15) * 128] = mrc[:, 2 * t]
        mr2[1, (t - 16) * 128:(t - 15) * 128] = mrc[:, 2 * t + 1]
    return {
        "ATT": np.asarray(rb["ATT"]), "XN": np.asarray(ra["XN"]), "MR2": mr2,
        "WOT": np.ascontiguousarray(np.asarray(inputs["L0_attn_w_out"], np.float32).T),
        "GW": np.asarray(inputs["L0_ffn_gate_W"], np.float32),
        "GB": np.asarray(inputs["L0_ffn_gate_b"], np.float32).reshape(4, 1),
        "W1": np.asarray(inputs["L0_ffn_W1"], np.float32),
        "W2": np.asarray(inputs["L0_ffn_W2"], np.float32),
        "B1T": _prm_t(inputs["L0_ffn_b1"]), "G1T": _prm_t(inputs["L0_ffn_g1"]),
        "BE1T": _prm_t(inputs["L0_ffn_be1"]), "B2T": _prm_t(inputs["L0_ffn_b2"]),
        "G2T": _prm_t(inputs["L0_ffn_g2"]), "BE2T": _prm_t(inputs["L0_ffn_be2"]),
        "IDENTF": np.eye(128, dtype=np.float32),
    }


def _cpk_host():
    e = np.arange(1024)[:, None].astype(np.float64)
    j = np.arange(1024)[None, :].astype(np.float64)
    ang = 2.0 * np.pi * e * j / 1024.0
    cpk = np.cos(ang)
    kk = 1024 - j
    angs = 2.0 * np.pi * e * kk / 1024.0
    sel = j > 512
    cpk = np.where(sel, -np.sin(angs), cpk)
    return np.ascontiguousarray(cpk.astype(np.float32))


_CPK = _cpk_host()
_TRIU = np.triu(np.ones((128, 128), np.float32)).astype(BF)


def _col128(a):
    return np.ascontiguousarray(np.asarray(a, np.float32).reshape(8, 128).T)


def _prep_l1a(inputs, b, h, rc):
    return {
        "X1": np.asarray(rc["X1"]),
        "WK": np.asarray(inputs["L1_attn_Wk"], np.float32),
        "WV": np.asarray(inputs["L1_attn_Wv"], np.float32),
        "BKC": _col128(inputs["L1_attn_bk"]),
        "BVC": _col128(inputs["L1_attn_bv"]),
        "CPK": _CPK, "TRIU": _TRIU,
    }


def _cinv_host():
    e = np.arange(1024)[None, :].astype(np.float64)
    j = np.arange(1024)[:, None].astype(np.float64)
    N = 1024.0
    out = np.zeros((1024, 1024), np.float64)
    out[0, :] = 1.0 / N
    for jj in range(1, 512):
        out[jj, :] = (2.0 / N) * np.cos(2.0 * np.pi * e[0] * jj / N)
    out[512, :] = (1.0 / N) * np.cos(np.pi * e[0])
    for jj in range(513, 1024):
        k = 1024 - jj
        out[jj, :] = -(2.0 / N) * np.sin(2.0 * np.pi * e[0] * k / N)
    return np.ascontiguousarray(out.astype(np.float32))


_CINV = _cinv_host()


def _prep_l1b(inputs, b, h, r1a_self, r1a_partner):
    if h == 1:
        totp = np.asarray(r1a_partner["TOT"], np.float32)
    else:
        totp = np.zeros((1, C), np.float32)
    return {
        "XN1": np.asarray(r1a_self["XN1"]),
        "WQ": np.asarray(inputs["L1_attn_Wq"], np.float32),
        "BQC": _col128(inputs["L1_attn_bq"]),
        "CPK": _CPK, "CINV": _CINV, "SLOC": np.asarray(r1a_self["SLOC"]),
        "TOTP": totp,
        "IDENT": np.eye(128, dtype=np.float32).astype(BF),
    }


def _prep_l1c(inputs, b, h, rc, r1a_self, r1a_partner, r1b_self):
    if h == 1:
        bnd = np.asarray(r1a_partner["XN1"])[:, -1]
    else:
        bnd = np.zeros((C,), np.float32).astype(BF)
    bnd = np.ascontiguousarray(bnd.reshape(8, 128).T)
    return {
        "X1": np.asarray(rc["X1"]),
        "XN1": np.asarray(r1a_self["XN1"]),
        "BND": bnd,
        "ATT1": np.asarray(r1b_self["ATT1"]),
        "MURC": _col128(inputs["L1_ffn_mu_r"]),
        "MUKC": _col128(inputs["L1_ffn_mu_k"]),
        "WRT": np.ascontiguousarray(np.asarray(inputs["L1_ffn_W_r"], np.float32).T),
        "WKT": np.ascontiguousarray(np.asarray(inputs["L1_ffn_W_k"], np.float32).T),
        "WVT2": np.ascontiguousarray(np.asarray(inputs["L1_ffn_W_v"], np.float32).T),
    }


def kernel(**inputs):
    nc_a = build_l0a()
    in_maps = [_prep_l0a(inputs, core // 2, core % 2) for core in range(8)]
    res_a = run_bass_kernel_spmd(nc_a, in_maps, core_ids=list(range(8)))
    nc_b = build_l0b()
    in_maps_b = [_prep_l0b(inputs, core // 2, core % 2, res_a.results[core])
                 for core in range(8)]
    res_b = run_bass_kernel_spmd(nc_b, in_maps_b, core_ids=list(range(8)))
    nc_c = build_l0c()
    in_maps_c = [_prep_l0c(inputs, core // 2, core % 2,
                           res_a.results[core], res_b.results[core])
                 for core in range(8)]
    res_c = run_bass_kernel_spmd(nc_c, in_maps_c, core_ids=list(range(8)))
    nc_1a = build_l1a()
    in_maps_1a = [_prep_l1a(inputs, core // 2, core % 2, res_c.results[core])
                  for core in range(8)]
    res_1a = run_bass_kernel_spmd(nc_1a, in_maps_1a, core_ids=list(range(8)))
    nc_1b = build_l1b()
    in_maps_1b = [_prep_l1b(inputs, core // 2, core % 2, res_1a.results[core],
                            res_1a.results[core - 1] if core % 2 == 1 else None)
                  for core in range(8)]
    res_1b = run_bass_kernel_spmd(nc_1b, in_maps_1b, core_ids=list(range(8)))
    nc_1c = build_l1c()
    in_maps_1c = [_prep_l1c(inputs, core // 2, core % 2, res_c.results[core],
                            res_1a.results[core],
                            res_1a.results[core - 1] if core % 2 == 1 else None,
                            res_1b.results[core])
                  for core in range(8)]
    res_1c = run_bass_kernel_spmd(nc_1c, in_maps_1c, core_ids=list(range(8)))
    out = np.empty((4, 4096, 1024), np.float32)
    for core in range(8):
        b, h = core // 2, core % 2
        out[b, h * 2048:(h + 1) * 2048, :] = np.asarray(
            res_1c.results[core]["XOUT"], np.float32).T
    return out
